# version 11
# baseline (speedup 1.0000x reference)
import numpy as np
import ml_dtypes
from contextlib import ExitStack

import concourse.bass as bass
import concourse.mybir as mybir
from concourse.bass_utils import run_bass_kernel_spmd

F32 = mybir.dt.float32
BF16 = mybir.dt.bfloat16
AF = mybir.ActivationFunctionType
ALU = mybir.AluOpType
AX = mybir.AxisListType

H = 1024
I2 = 2048
EXP = 8
FF = 2048
KCONV = 4
NGRP = 8
KT_H = H // 128
KT_I = I2 // 128
EPS = 1e-5

ENGS = ("pe", "act", "dve", "sp", "pool")


class Prog:

    def __init__(self):
        self.items = {e: [] for e in ENGS}
        self.cnt = dict.fromkeys(ENGS, 0)
        self.qcnt = {}
        self.res = {}
        self.clock = {s: {} for s in ENGS}
        self.opclock = {}
        self.sems = {}
        self.chan_warn = set()
        self.nwaits = dict.fromkeys(ENGS, 0)
        self.nfns = dict.fromkeys(ENGS, 0)

    @staticmethod
    def keys(name, c0, c1):
        return [(name, b) for b in range(c0 // 512, (c1 + 511) // 512)]

    def _deps(self, reads, writes):
        deps = []
        for k in reads:
            ent = self.res.get(k)
            if ent and ent[0] is not None:
                deps.append(ent[0])
        for k in writes:
            ent = self.res.get(k)
            if ent:
                if ent[0] is not None:
                    deps.append(ent[0])
                deps.extend(ent[1])
        return deps

    def _need(self, eng, deps):
        need = {}
        clk = self.clock[eng]
        for (s, i) in deps:
            if clk.get(s, 0) >= i:
                continue
            if need.get(s, 0) < i:
                need[s] = i
        return need

    def _merge(self, eng, need):
        clk = self.clock[eng]
        for s, i in need.items():
            oc = self.opclock.get((s, i), {})
            for s2, i2 in oc.items():
                if clk.get(s2, 0) < i2:
                    clk[s2] = i2
            if clk.get(s, 0) < i:
                clk[s] = i

    def _update_res(self, op, reads, writes):
        wset = set(writes)
        for k in reads:
            if k in wset:
                continue
            ent = self.res.get(k)
            if ent is None:
                ent = (None, [])
                self.res[k] = ent
            ent[1].append(op)
        for k in writes:
            self.res[k] = (op, [])

    def emit(self, eng, fns, reads=(), writes=()):
        if not isinstance(fns, (list, tuple)):
            fns = [fns]
        need = self._need(eng, self._deps(reads, writes))
        self._merge(eng, need)
        self.cnt[eng] += 1
        idx = self.cnt[eng]
        op = (eng, idx)
        oc = dict(self.clock[eng])
        oc[eng] = idx
        self.opclock[op] = oc
        self.nwaits[eng] += len(need)
        self.nfns[eng] += len(fns)
        prog = self

        def closure(h, need=need, fns=fns, eng=eng):
            for s, i in sorted(need.items()):
                h.wait_ge(prog.sems[s], i * (16 if s.startswith("q_") else 1))
            last = None
            for f in fns:
                last = f(h)
            last.then_inc(prog.sems[eng], 1)
        self.items[eng].append(closure)
        self._update_res(op, reads, writes)
        return op

    def dma(self, eng, chan, fn, reads=(), writes=()):
        q = "q_" + chan
        if q not in self.qcnt:
            self.qcnt[q] = 0
            self.clock[q] = {}
        need = self._need(eng, self._deps(reads, writes))
        self._merge(eng, need)
        prev = self.qcnt[q]
        if prev > 0 and self.clock[eng].get(q, 0) < prev and q not in self.chan_warn:
            self.chan_warn.add(q)
            print(f"WARNING: dma channel {chan} may have >1 outstanding at "
                  f"issue #{prev + 1}")
        self.qcnt[q] += 1
        qidx = self.qcnt[q]
        qop = (q, qidx)
        oc = dict(self.clock[eng])
        oc[q] = qidx
        self.opclock[qop] = oc
        prog = self

        def closure(h, need=need, fn=fn, q=q):
            for s, i in sorted(need.items()):
                h.wait_ge(prog.sems[s], i * (16 if s.startswith("q_") else 1))
            fn(h).then_inc(prog.sems[q], 16)
        self.items[eng].append(closure)
        self._update_res(qop, reads, writes)
        return qop

    def flush(self, nc, ctx):
        for s in list(ENGS) + list(self.qcnt.keys()):
            self.sems[s] = ctx.enter_context(nc.semaphore(f"sem_{s}"))
        block = ctx.enter_context(nc.Block())
        for eng, attr in (("pe", "tensor"), ("act", "scalar"), ("dve", "vector"),
                          ("sp", "sync"), ("pool", "gpsimd")):
            items = self.items[eng]
            if items:
                getattr(block, attr)(lambda h, items=items: [f(h) for f in items])


def build_program(T=4096, NC=1024, taps=(), tap_chunk=0, tap_expert=0):
    NT = NC // 512
    TT = NC // 128
    CH = T // NC
    W = NT * 512
    assert NC % 512 == 0 and T % NC == 0 and NT in (1, 2)

    nc = bass.Bass("TRN2", target_bir_lowering=False)

    def din(name, shape, dt=F32):
        return nc.dram_tensor(name, shape, dt, kind="ExternalInput").ap()

    x = din("x", [T, H])
    xh = din("xh", [3, H])
    w_inproj = din("L0_attn_in_proj_w", [2 * I2, H])
    w_conv = din("L0_attn_conv_w", [I2, I2 // NGRP, KCONV])
    b_conv = din("L0_attn_conv_b", [I2])
    ln_g = din("L0_attn_ln_g", [I2])
    ln_b = din("L0_attn_ln_b", [I2])
    w_fc = din("L0_attn_fc_w", [H, I2])
    b_fc = din("L0_attn_fc_b", [H])
    w_cproj = din("L0_attn_cproj_w", [I2, H])
    b_cproj = din("L0_attn_cproj_b", [I2])
    w_out = din("L0_attn_out_w", [H, I2])
    w_gate = din("L0_ffn_Wgate", [H, EXP])
    b_gate = din("L0_ffn_bgate", [EXP])
    w1 = din("L0_ffn_W1", [EXP, H, FF])
    b1 = din("L0_ffn_b1", [EXP, FF])
    w2 = din("L0_ffn_W2", [EXP, H, FF])
    b2 = din("L0_ffn_b2", [EXP, FF])
    wg = din("L0_ffn_Wg", [EXP, FF, FF])
    bg = din("L0_ffn_bg", [EXP, FF])
    wl = din("L0_ffn_Wl", [EXP, FF, FF])
    bl = din("L0_ffn_bl", [EXP, FF])
    w3 = din("L0_ffn_W3", [EXP, FF, H])
    b3 = din("L0_ffn_b3", [EXP, H])
    onesf = din("onesf", [128, 512])
    identf_in = din("identf", [128, 128])
    sel8_in = din("sel8", [8, 1024])

    out = nc.dram_tensor("out", [T, H], F32, kind="ExternalOutput").ap()

    scr_inproj = nc.dram_tensor("scr_inproj", [16, 128, KT_H * 256], BF16).ap()
    scr_fc = nc.dram_tensor("scr_fc", [8, 128, KT_I * 128], BF16).ap()
    scr_cproj = nc.dram_tensor("scr_cproj", [8, 128, KT_H * 256], BF16).ap()
    scr_outw = nc.dram_tensor("scr_outw", [8, 128, KT_I * 128], BF16).ap()
    scr_conv = nc.dram_tensor("scr_conv", [16, 128, 8 * 128], BF16).ap()
    scr_w12 = nc.dram_tensor("scr_w12", [EXP * 2 * 8, 128, 2048], BF16).ap()
    scr_wgl = nc.dram_tensor("scr_wgl", [EXP * 2 * 16, 128, 2048], BF16).ap()
    scr_w3 = nc.dram_tensor("scr_w3", [EXP * 8, 128, 2048], BF16).ap()

    tap_outs = {}

    ctx = ExitStack()
    with ctx:
        def sbt(name, shape, dt):
            return ctx.enter_context(nc.sbuf_tensor(name, shape, dt))

        xnb = sbt("xnb", [128, KT_H * NC], BF16)
        hz = sbt("hz", [128, KT_I * (NC + 3)], BF16)
        cbuf = sbt("cbuf", [128, KT_I * NC], BF16)
        fbuf = sbt("fbuf", [128, max(KT_I * 512, EXP * NC, KT_H * NC)], BF16)
        sb_s = sbt("sb_s", [128, KT_H * NC], BF16)
        yb = sbt("yb", [128, KT_H * NC], BF16)
        gw = sbt("gw", [128, 4 * 2048], BF16)
        xt = sbt("xt", [128, 2 * H], F32)
        ot = sbt("ot", [128, 2 * H], F32)
        xtm = sbt("xtm", [128, 2 * H], BF16)
        escr = sbt("escr", [128, 3 * 1024], BF16)
        bias_f = sbt("bias_f", [128, 652], F32)
        rmb = sbt("rmb", [128, 2 * NC], BF16)
        rows = sbt("rows", [1, 2 * NC], F32)
        rows_bf = sbt("rows_bf", [1, NC], BF16)
        gout_fm = sbt("gout_fm", [8, NC], BF16)
        gtw = sbt("gtw", [128, 8 * TT], F32)
        gops = sbt("gops", [128, 32], F32)
        gsc = sbt("gsc", [128, 8], F32)
        combtm = sbt("combtm", [128, 8], BF16)
        comb8 = sbt("comb8", [8, NC], BF16)
        selb = sbt("selb", [8, 1024], BF16)
        wgres = sbt("wgres", [128, KT_H * 8], BF16)
        bgate_row = sbt("bgate_row", [1, 8], BF16)
        convbrow = sbt("convbrow", [1, I2], BF16)
        onesb = sbt("onesb", [128, 512], BF16)
        identb = sbt("identb", [128, 128], BF16)
        identf = sbt("identf_s", [128, 128], F32)
        stat = sbt("stat", [128, 16], F32)
        xnh_T = sbt("xnh_T", [128, 24], BF16)
        htail = sbt("htail", [128, KT_I * 3], BF16)
        bwork = sbt("bwork", [128, 128], F32)
        pland = sbt("pland", [128, 2048], F32)

        pbig = ctx.enter_context(nc.psum_tensor("pbig", [128, 6 * 512], F32))
        pTT = ctx.enter_context(nc.psum_tensor("pTT", [128, 2 * 1024], BF16))

        def pbank(b, n0=0, nn=1):
            return pbig[:, (b + n0) * 512:(b + n0 + nn) * 512]

        def pkeys(b, nn=1):
            return [("pb", b + i) for i in range(nn)]

        def ptb(t):
            return pTT[:, t * 1024:(t + 1) * 1024]

        def ptk(t):
            return [("pT", t)]

        pr = Prog()
        bk = Prog.keys

        esl = [0]

        def escr_slot():
            s = esl[0] % 3
            esl[0] += 1
            return escr[:, s * 1024:s * 1024 + W], bk("escr", s * 1024, s * 1024 + W)

        pr.dma("pool", "ini0", lambda h: h.dma_start(out=onesb[:, :], in_=onesf[:, :]),
               writes=bk("onesb", 0, 512))
        pr.dma("pool", "ini1", lambda h: h.dma_start(out=identb[:, :], in_=identf_in[:, :]),
               writes=bk("identb", 0, 128))
        pr.dma("sp", "ini2", lambda h: h.dma_start(out=identf[:, :], in_=identf_in[:, :]),
               writes=bk("identf_s", 0, 128))
        ONES_COL = onesb[:, 0:1]
        ONES_ROW128 = onesb[0:1, 0:128]
        ONES_ROW512 = onesb[0:1, 0:512]
        ONES_KEYS = bk("onesb", 0, 512)
        IDB_KEYS = bk("identb", 0, 128)
        IDF_KEYS = bk("identf_s", 0, 128)

        pr.dma("pool", "ini3", lambda h: h.dma_start(
            out=wgres[:, :].rearrange("p (kt m) -> p kt m", kt=KT_H),
            in_=w_gate[:, :].rearrange("(kt p) m -> p kt m", p=128)),
            writes=bk("wgres", 0, KT_H * 8))
        pr.dma("pool", "ini4", lambda h: h.dma_start(out=bgate_row[:, :],
                                                     in_=b_gate[None, :]),
               writes=bk("bgate_row", 0, 8))
        pr.dma("pool", "ini5", lambda h: h.dma_start(out=selb[:, :], in_=sel8_in[:, :]),
               writes=bk("selb", 0, 1024))
        pr.dma("pool", "ini6", lambda h: h.dma_start(
            out=convbrow[:, :], in_=b_conv[None, :]), writes=bk("convbrow", 0, I2))
        pr.emit("dve", [lambda v: v.memset(bias_f[:, 648:649], EPS)],
                writes=bk("bias_f", 648, 649))

        def mm_emit(pairs_banks, reads, wkeys):
            assert len(pairs_banks) == len(wkeys)
            for (out_ap, pairs), wk in zip(pairs_banks, wkeys):
                npair = len(pairs)
                fns = [lambda t, l=l, r=r, i=i, n=npair, o=out_ap:
                       t.matmul(o, l, r, start=(i == 0), stop=(i == n - 1))
                       for i, (l, r) in enumerate(pairs)]
                pr.emit("pe", fns, reads=reads, writes=[wk])

        def tr_group(tb, triples, reads, np_=128):
            fns = [lambda t, o=o, s=s, np_=np_: t.transpose(o, s, identb[0:np_, 0:np_])
                   for (o, s) in triples]
            return pr.emit("pe", fns, reads=list(reads) + IDB_KEYS, writes=ptk(tb))

        def load_slab(qeng, slot, ncols, src_ap, src_keys=(), chan=None):
            c0 = slot * 2048
            ch = chan or f"w{slot}"
            pr.dma(qeng, ch,
                   lambda h, d=gw[:, c0:c0 + ncols], s=src_ap: h.dma_start(out=d, in_=s),
                   reads=src_keys, writes=bk("gw", c0, c0 + ncols))
            return c0

        def land_slot(p, ncols):
            if p == 0:
                return pland[:, 0:ncols], bk("pland", 0, ncols)
            return xt[:, 0:ncols], bk("xt", 0, ncols)

        def convert_into_slot(slot, src_f32_3d, scr, scr_name, idx, ncols=2048):
            p = slot % 2
            land, lkeys = land_slot(p, ncols)
            pr.dma("act", f"cl{p}", lambda h, d=land, sap=src_f32_3d:
                   h.dma_start(out=d, in_=sap), writes=lkeys)
            c0 = slot * 2048
            gk = bk("gw", c0, c0 + ncols)
            pr.emit("dve", [lambda v, c0=c0, ncols=ncols, land=land:
                            v.tensor_copy(gw[:, c0:c0 + ncols], land)],
                    reads=lkeys, writes=gk)
            pr.dma("sp", f"cs{slot}", lambda h, sidx=idx, c0=c0, scr=scr, nc_=ncols:
                   h.dma_start(out=scr[sidx, :, 0:nc_], in_=gw[:, c0:c0 + nc_]),
                   reads=gk, writes=[(scr_name, idx)])

        def bias_pack_f32(specs, dst_c0, width):
            for (src, r0, nr) in specs:
                pr.dma("act", "bias", lambda h, s=src, r0=r0, nr=nr:
                       h.dma_start(out=bwork[r0:r0 + nr, 0:128], in_=s),
                       writes=bk("bwork", 0, 128))
            pr.emit("pe", [lambda t: t.transpose(pbank(5)[:, 0:128], bwork[:, 0:128],
                                                 identf[:, :])],
                    reads=bk("bwork", 0, 128) + IDF_KEYS, writes=pkeys(5))
            pr.emit("dve", [lambda v, c0=dst_c0, w=width:
                            v.tensor_copy(bias_f[:, c0:c0 + w], pbank(5)[:, 0:w])],
                    reads=pkeys(5), writes=bk("bias_f", dst_c0, dst_c0 + width))

        for epair in range(4):
            e0 = 2 * epair
            specs = []
            r = 0
            for e in (e0, e0 + 1):
                for vec in (b1, b2, bg, bl):
                    specs.append((vec[e, :].rearrange("(q p) -> q p", p=128), r, 16))
                    r += 16
            bias_pack_f32(specs, e0 * 64, 128)
        bias_pack_f32([(b3[e, :].rearrange("(q p) -> q p", p=128), e * 8, 8)
                       for e in range(EXP)], 512, 64)
        bias_pack_f32([
            (b_conv[:].rearrange("(q p) -> q p", p=128), 0, 16),
            (ln_g[:].rearrange("(q p) -> q p", p=128), 16, 16),
            (ln_b[:].rearrange("(q p) -> q p", p=128), 32, 16),
            (b_cproj[:].rearrange("(q p) -> q p", p=128), 48, 16),
        ], 576, 64)
        bias_pack_f32([(b_fc[:].rearrange("(q p) -> q p", p=128), 0, 8)], 640, 8)

        def bias_ap(col):
            return bias_f[:, col:col + 1], bk("bias_f", col, col + 1)

        B1C = lambda e, m: e * 64 + m
        B2C = lambda e, m: e * 64 + 16 + m
        BGC = lambda e, m: e * 64 + 32 + m
        BLC = lambda e, m: e * 64 + 48 + m
        B3C = lambda e, m: 512 + e * 8 + m
        LNGC = lambda m: 592 + m
        LNBC = lambda m: 608 + m
        CPROJBC = lambda m: 624 + m
        FCBC = lambda m: 640 + m
        EPSK = bk("bias_f", 648, 649)

        def prep_layer(w_ap, OUT, IN, scr, scr_name, SW):
            KT = IN // 128
            MMs = SW // 128
            for s in range(OUT // SW):
                for mmi in range(MMs):
                    ob = s * MMs + mmi
                    land, lkeys = land_slot(0, IN)
                    pr.dma("act", "cl0", lambda h, ob=ob, d=land, w_ap=w_ap:
                           h.dma_start(out=d, in_=w_ap[ob * 128:(ob + 1) * 128, :]),
                           writes=lkeys)
                    lc0 = 2 * 2048
                    pr.emit("dve", [lambda v, lc0=lc0, IN=IN, land=land:
                                    v.tensor_copy(gw[:, lc0:lc0 + IN], land)],
                            reads=lkeys, writes=bk("gw", lc0, lc0 + IN))
                    for kth in range(0, KT, 8):
                        kn = min(8, KT - kth)
                        tb = (kth // 8) % 2
                        tr_group(tb, [(ptb(tb)[:, i * 128:(i + 1) * 128],
                                       gw[:, lc0 + (kth + i) * 128:
                                          lc0 + (kth + i + 1) * 128])
                                      for i in range(kn)],
                                 bk("gw", lc0 + kth * 128, lc0 + (kth + kn) * 128))
                        sc = 3 * 2048 + kth * SW + mmi * 128
                        dst = gw[:, 3 * 2048:4 * 2048].rearrange(
                            "p (kt w) -> p kt w", w=SW)[:, kth:kth + kn,
                                                        mmi * 128:(mmi + 1) * 128]
                        pr.emit("dve", [lambda v, tb=tb, dst=dst, kn=kn:
                                        v.tensor_copy(dst, ptb(tb)[:, 0:kn * 128]
                                                      .rearrange("p (kt w) -> p kt w",
                                                                 w=128))],
                                reads=ptk(tb),
                                writes=bk("gw", 3 * 2048 + kth * SW,
                                          3 * 2048 + (kth + kn) * SW))
                pr.dma("sp", "scr", lambda h, s=s, KT=KT, SW=SW, scr=scr:
                       h.dma_start(out=scr[s, :, :],
                                   in_=gw[:, 3 * 2048:3 * 2048 + KT * SW]),
                       reads=bk("gw", 3 * 2048, 3 * 2048 + KT * SW),
                       writes=[(scr_name, s)])

        def prep_conv():
            for g in range(NGRP):
                for otile in range(2):
                    m = 2 * g + otile
                    lc0 = 2 * 2048
                    land, lkeys = land_slot(0, 1024)
                    pr.dma("act", "cl0", lambda h, g=g, otile=otile, d=land:
                           h.dma_start(out=d,
                                       in_=w_conv[g * 256 + otile * 128:
                                                  g * 256 + (otile + 1) * 128, :, :]
                                       .rearrange("p ci k -> p (ci k)")),
                           writes=lkeys)
                    pr.emit("dve", [lambda v, lc0=lc0, land=land:
                                    v.tensor_copy(gw[:, lc0:lc0 + 1024], land)],
                            reads=lkeys, writes=bk("gw", lc0, lc0 + 1024))
                    for c2 in range(2):
                        tb = c2 % 2
                        tr_group(tb, [(ptb(tb)[:, tp * 128:(tp + 1) * 128],
                                       gw[:, lc0 + c2 * 512:lc0 + (c2 + 1) * 512]
                                       .rearrange("p (ci k) -> p k ci",
                                                  k=KCONV)[:, tp, :])
                                      for tp in range(KCONV)],
                                 bk("gw", lc0 + c2 * 512, lc0 + (c2 + 1) * 512))
                        j0 = c2 * KCONV
                        pr.emit("dve", [lambda v, tb=tb, j0=j0: v.tensor_copy(
                            gw[:, 3 * 2048 + j0 * 128:3 * 2048 + (j0 + 4) * 128],
                            ptb(tb)[:, 0:512])],
                            reads=ptk(tb),
                            writes=bk("gw", 3 * 2048 + j0 * 128,
                                      3 * 2048 + (j0 + 4) * 128))
                    pr.dma("sp", "scr", lambda h, m=m:
                           h.dma_start(out=scr_conv[m, :, :],
                                       in_=gw[:, 3 * 2048:3 * 2048 + 1024]),
                           reads=bk("gw", 3 * 2048, 3 * 2048 + 1024),
                           writes=[("scr_conv", m)])

        SK = bk("stat", 0, 16)
        pr.dma("act", "x0", lambda h: h.dma_start(out=xt[0:3, 0:H], in_=xh[:, :]),
               writes=bk("xt", 0, H))

        def ln1_chain(rs, xsl, xkeys, osl, okeys, tmsl, tmkeys):
            pr.emit("dve", [lambda v: v.scalar_tensor_tensor(
                osl, xsl, 1.0, xsl, ALU.mult, ALU.mult, accum_out=stat[rs, 1:2])],
                reads=xkeys, writes=okeys + SK)
            pr.emit("dve", [lambda v: v.reduce_sum(stat[rs, 0:1], xsl, axis=AX.X)],
                    reads=xkeys, writes=SK)
            pr.emit("dve", [lambda v: v.tensor_scalar(stat[rs, 2:3], stat[rs, 0:1],
                                                      1.0 / H, None, ALU.mult)],
                    reads=SK, writes=SK)
            pr.emit("dve", [lambda v: v.tensor_tensor(stat[rs, 4:5], stat[rs, 2:3],
                                                      stat[rs, 2:3], ALU.mult)],
                    reads=SK, writes=SK)
            pr.emit("dve", [lambda v: v.scalar_tensor_tensor(
                stat[rs, 5:6], stat[rs, 1:2], 1.0 / H, stat[rs, 4:5],
                ALU.mult, ALU.subtract)],
                reads=SK, writes=SK)
            pr.emit("act", [lambda a: a.activation(stat[rs, 6:7], stat[rs, 5:6],
                                                   AF.Sqrt,
                                                   bias=bias_f[rs, 648:649],
                                                   scale=1.0)],
                    reads=SK + EPSK, writes=SK)
            pr.emit("dve", [lambda v: v.reciprocal(stat[rs, 7:8], stat[rs, 6:7])],
                    reads=SK, writes=SK)
            pr.emit("dve", [lambda v: v.tensor_scalar(
                tmsl, xsl, stat[rs, 2:3], stat[rs, 7:8], ALU.subtract, ALU.mult)],
                reads=xkeys + SK, writes=tmkeys)

        ln1_chain(slice(0, 3), xt[0:3, 0:H], bk("xt", 0, H), ot[0:3, 0:H],
                  bk("ot", 0, H), xtm[0:3, 0:H], bk("xtm", 0, H))
        tr_group(0, [(ptb(0)[:, kt * 4:kt * 4 + 3], xtm[0:3, kt * 128:(kt + 1) * 128])
                     for kt in range(KT_H)], bk("xtm", 0, H), np_=3)
        pr.emit("dve", [lambda v: v.tensor_copy(
            xnh_T[:, :].rearrange("p (kt s) -> p kt s", s=3),
            ptb(0)[:, 0:32].rearrange("p (kt s) -> p kt s", s=4)[:, :, 0:3])],
            reads=ptk(0), writes=bk("xnh_T", 0, 24))
        XNH_KEYS = bk("xnh_T", 0, 24)

        def hzc(kt):
            return kt * (NC + 3)

        def tap(name, ap, rows_n, cols, dt, keys):
            if name not in taps:
                return
            tname = "dbg_" + name
            if tname not in tap_outs:
                tap_outs[tname] = nc.dram_tensor(tname, [rows_n, cols], dt,
                                                 kind="ExternalOutput").ap()
            pr.dma("sp", "tp_" + name,
                   lambda h, d=tap_outs[tname], s=ap: h.dma_start(out=d, in_=s),
                   reads=keys, writes=[("OUTD_" + tname, 0)])

        out_keys_all = []
        HZK_ALL = [k for kt in range(KT_I) for k in bk("hz", hzc(kt), hzc(kt) + NC + 3)]

        for c in range(CH):
            t0 = c * NC
            tap_here = (c == tap_chunk)
            NS = 2 if c == 0 else 4

            for tt in range(TT):
                xs = tt % 2
                xsl = xt[:, xs * H:(xs + 1) * H]
                xkeys = bk("xt", xs * H, (xs + 1) * H)
                pr.dma("act", f"x{xs}", lambda h, r0=t0 + tt * 128, d=xsl:
                       h.dma_start(out=d, in_=x[r0:r0 + 128, :]), writes=xkeys)
                osl = ot[:, xs * H:(xs + 1) * H]
                okeys = bk("ot", xs * H, (xs + 1) * H)
                tsl = xtm[:, xs * H:(xs + 1) * H]
                tkeys = bk("xtm", xs * H, (xs + 1) * H)
                ln1_chain(slice(None), xsl, xkeys, osl, okeys, tsl, tkeys)
                tb = tt % 2
                tr_group(tb, [(ptb(tb)[:, kt * 128:(kt + 1) * 128],
                               tsl[:, kt * 128:(kt + 1) * 128])
                              for kt in range(KT_H)], tkeys)
                dst = xnb[:, :].rearrange("p (kt t) -> p kt t",
                                          t=NC)[:, :, tt * 128:(tt + 1) * 128]
                wkeys = [k for kt in range(KT_H)
                         for k in bk("xnb", kt * NC + tt * 128,
                                     kt * NC + (tt + 1) * 128)]
                pr.emit("dve", [lambda v, tb=tb, dst=dst: v.tensor_copy(
                    dst, ptb(tb)[:, 0:1024].rearrange("p (kt t) -> p kt t", t=128))],
                    reads=ptk(tb), writes=wkeys)

            def xn_ap(kt, n0):
                return xnb[:, kt * NC + n0 * 512:kt * NC + n0 * 512 + 512]

            def xn_keys(kt):
                return bk("xnb", kt * NC, kt * NC + NC)

            XNK_ALL = [k for kt in range(KT_H) for k in xn_keys(kt)]

            if tap_here:
                tap("xn", xnb[:, :], 128, KT_H * NC, BF16, bk("xnb", 0, KT_H * NC))

            if c == 0:
                prep_layer(w_inproj, 2 * I2, H, scr_inproj, "scr_inproj", 256)

            if c > 0:
                pr.emit("dve", [lambda v: v.tensor_copy(
                    hz[:, :].rearrange("p (kt q) -> p kt q", q=NC + 3)[:, :, 0:3],
                    htail[:, :].rearrange("p (kt s) -> p kt s", s=3))],
                    reads=bk("htail", 0, KT_I * 3),
                    writes=[k for kt in range(KT_I)
                            for k in bk("hz", hzc(kt), hzc(kt) + 3)])

            nslabs = (2 * I2) // 256
            for s in range(min(NS, nslabs)):
                load_slab("sp", s % NS, 2048, scr_inproj[s, :, :],
                          [("scr_inproj", s)], chan=f"a{s % NS}")
            for s in range(nslabs):
                sc0 = (s % NS) * 2048
                for mmi in range(2):
                    m = 2 * s + mmi
                    bp = (m % 3) * 2
                    groups = []
                    for n in range(NT):
                        pairs = [(gw[:, sc0 + kt * 256 + mmi * 128:
                                     sc0 + kt * 256 + mmi * 128 + 128],
                                  xn_ap(kt, n)) for kt in range(KT_H)]
                        groups.append((pbank(bp + n), pairs))
                    rd = bk("gw", sc0, sc0 + 2048) + XNK_ALL
                    wk = pkeys(bp, NT)
                    if c == 0 and m < KT_I:
                        hp = [(gw[:, sc0 + kt * 256 + mmi * 128:
                                  sc0 + kt * 256 + mmi * 128 + 128],
                               xnh_T[:, kt * 3:(kt + 1) * 3]) for kt in range(KT_H)]
                        hb = ((m % 3) * 2 + 2) % 6
                        groups.append((pbank(hb)[:, 0:3], hp))
                        rd = rd + XNH_KEYS
                        wk = wk + pkeys(hb)
                        mm_emit(groups, rd, wk)
                        d0 = hzc(m)
                        pr.emit("dve", [lambda v, hb=hb, d0=d0: v.tensor_copy(
                            hz[:, d0:d0 + 3], pbank(hb)[:, 0:3])],
                            reads=pkeys(hb), writes=bk("hz", d0, d0 + 3))
                    else:
                        mm_emit(groups, rd, wk)
                    if m < KT_I:
                        d0 = hzc(m) + 3
                        pr.emit("dve", [lambda v, bp=bp, d0=d0: v.tensor_copy(
                            hz[:, d0:d0 + W], pbank(bp, 0, NT))],
                            reads=pkeys(bp, NT), writes=bk("hz", d0, d0 + W))
                    else:
                        mg = m - KT_I
                        d0 = mg * NC
                        pr.emit("act", [lambda a, bp=bp, d0=d0: a.activation(
                            cbuf[:, d0:d0 + W], pbank(bp, 0, NT), AF.Silu)],
                            reads=pkeys(bp, NT), writes=bk("cbuf", d0, d0 + W))
                if s + NS < nslabs:
                    load_slab("sp", (s + NS) % NS, 2048, scr_inproj[s + NS, :, :],
                              [("scr_inproj", s + NS)], chan=f"a{(s + NS) % NS}")
            if tap_here:
                tap("h", hz[:, :], 128, KT_I * (NC + 3), BF16, HZK_ALL)
                tap("gate", cbuf[:, :], 128, KT_I * NC, BF16, bk("cbuf", 0, KT_I * NC))

            if c == 0:
                prep_conv()

            if c + 1 < CH:
                rdk = [k for kt in range(KT_I)
                       for k in bk("hz", hzc(kt) + NC, hzc(kt) + NC + 3)]
                pr.emit("dve", [lambda v: v.tensor_copy(
                    htail[:, :].rearrange("p (kt s) -> p kt s", s=3),
                    hz[:, :].rearrange("p (kt q) -> p kt q",
                                       q=NC + 3)[:, :, NC:NC + 3])],
                    reads=rdk, writes=bk("htail", 0, KT_I * 3))
            for m in range(min(NS, KT_I)):
                load_slab("sp", m % NS, 1024, scr_conv[m, :, :], [("scr_conv", m)],
                          chan=f"a{m % NS}")
            for g in range(NGRP):
                ms = (2 * g, 2 * g + 1)
                for mi, m in enumerate(ms):
                    sc0 = (m % NS) * 2048
                    bp = (m % 3) * 2
                    groups = []
                    for n in range(NT):
                        pairs = []
                        for c2 in range(2):
                            for tp in range(KCONV):
                                j = c2 * KCONV + tp
                                r0 = hzc(2 * g + c2) + n * 512 + tp
                                pairs.append((gw[:, sc0 + j * 128:sc0 + (j + 1) * 128],
                                              hz[:, r0:r0 + 512]))
                        pairs.append((convbrow[0:1, m * 128:(m + 1) * 128],
                                      ONES_ROW512))
                        groups.append((pbank(bp + n), pairs))
                    rd = bk("gw", sc0, sc0 + 1024)
                    rd += bk("hz", hzc(2 * g), hzc(2 * g) + NC + 3)
                    rd += bk("hz", hzc(2 * g + 1), hzc(2 * g + 1) + NC + 3)
                    rd += bk("convbrow", m * 128, (m + 1) * 128) + list(ONES_KEYS)
                    mm_emit(groups, rd, pkeys(bp, NT))
                for mi, m in enumerate(ms):
                    bp = (m % 3) * 2
                    g0 = m * NC
                    d0 = hzc(m) + 3
                    pr.emit("dve", [lambda v, bp=bp, d0=d0, g0=g0: v.tensor_tensor(
                        hz[:, d0:d0 + W], pbank(bp, 0, NT), cbuf[:, g0:g0 + W],
                        ALU.mult)],
                        reads=pkeys(bp, NT) + bk("cbuf", g0, g0 + W),
                        writes=bk("hz", d0, d0 + W))
                for mn in (2 * g + NS, 2 * g + NS + 1):
                    if mn < KT_I:
                        load_slab("sp", mn % NS, 1024, scr_conv[mn, :, :],
                                  [("scr_conv", mn)], chan=f"a{mn % NS}")
            if tap_here:
                tap("h2", hz[:, :], 128, KT_I * (NC + 3), BF16, HZK_ALL)

            def do_p7a():
                WGK = bk("wgres", 0, KT_H * 8)
                BGRK = bk("bgate_row", 0, 8)
                GFK = bk("gout_fm", 0, NC)
                groups = []
                for n in range(NT):
                    pairs = [(wgres[:, kt * 8:(kt + 1) * 8], xn_ap(kt, n))
                             for kt in range(KT_H)]
                    pairs.append((bgate_row[0:1, :], ONES_ROW512))
                    groups.append((pbank(4 + n)[0:8, :], pairs))
                mm_emit(groups, list(WGK) + list(BGRK) + list(ONES_KEYS) + XNK_ALL,
                        pkeys(4, NT))
                pr.emit("dve", [lambda v: v.tensor_copy(
                    gout_fm[0:8, 0:W], pbank(4, 0, NT)[0:8, :])],
                    reads=pkeys(4, NT), writes=GFK)
                tr_group(0, [(ptb(0)[:, tt * 8:(tt + 1) * 8],
                              gout_fm[0:8, tt * 128:(tt + 1) * 128])
                             for tt in range(TT)], GFK, np_=8)
                pr.emit("dve", [lambda v: v.tensor_copy(
                    gtw[:, 0:8 * TT], ptb(0)[:, 0:8 * TT])],
                    reads=ptk(0), writes=bk("gtw", 0, 8 * TT))
                GTK = bk("gtw", 0, 8 * TT)
                GOK = bk("gops", 0, 32)
                GSK = bk("gsc", 0, 8)
                CTK = bk("combtm", 0, 8)
                for tt in range(TT):
                    gsl = gtw[:, tt * 8:(tt + 1) * 8]
                    pr.emit("dve", [lambda v, gsl=gsl: v.reduce_max(
                        gsc[:, 0:1], gsl, axis=AX.X)], reads=GTK, writes=GSK)
                    pr.emit("dve", [lambda v, gsl=gsl: v.tensor_scalar(
                        gops[:, 0:8], gsl, gsc[:, 0:1], None, ALU.is_ge)],
                        reads=GTK + GSK, writes=GOK)
                    pr.emit("dve", [lambda v, gsl=gsl: v.scalar_tensor_tensor(
                        gops[:, 8:16], gops[:, 0:8], -1e30, gsl, ALU.mult, ALU.add)],
                        reads=GOK + GTK, writes=GOK)
                    pr.emit("dve", [lambda v: v.reduce_max(gsc[:, 1:2], gops[:, 8:16],
                                                           axis=AX.X)],
                            reads=GOK, writes=GSK)
                    pr.emit("dve", [lambda v: v.tensor_scalar(gops[:, 16:24],
                                                              gops[:, 8:16],
                                                              gsc[:, 1:2], None,
                                                              ALU.is_ge)],
                            reads=GOK + GSK, writes=GOK)
                    pr.emit("dve", [lambda v: v.tensor_scalar(gsc[:, 2:3],
                                                              gsc[:, 1:2], -1.0, None,
                                                              ALU.mult)],
                            reads=GSK, writes=GSK)
                    pr.emit("act", [lambda a: a.activation(gsc[:, 3:4], gsc[:, 0:1],
                                                           AF.Sigmoid,
                                                           bias=gsc[:, 2:3],
                                                           scale=1.0)],
                            reads=GSK, writes=GSK)
                    pr.emit("dve", [lambda v: v.tensor_scalar(gsc[:, 4:5],
                                                              gsc[:, 3:4], -1.0, 1.0,
                                                              ALU.mult, ALU.add)],
                            reads=GSK, writes=GSK)
                    pr.emit("dve", [lambda v: v.tensor_scalar(gops[:, 24:32],
                                                              gops[:, 0:8],
                                                              gsc[:, 3:4], None,
                                                              ALU.mult)],
                            reads=GOK + GSK, writes=GOK)
                    pr.emit("dve", [lambda v: v.scalar_tensor_tensor(
                        combtm[:, :], gops[:, 16:24], gsc[:, 4:5], gops[:, 24:32],
                        ALU.mult, ALU.add)],
                        reads=GOK + GSK, writes=CTK)
                    tr_group(1, [(ptb(1)[0:8, tt * 128:(tt + 1) * 128],
                                  combtm[:, 0:8])], CTK)
                pr.emit("dve", [lambda v: v.tensor_copy(comb8[0:8, :],
                                                        ptb(1)[0:8, 0:NC])],
                        reads=ptk(1), writes=bk("comb8", 0, NC))

            do_p7a()

            if c == 0:
                prep_layer(w_fc, H, I2, scr_fc, "scr_fc", 128)

            RK = bk("rows", 0, 2 * NC)
            RBK = bk("rows_bf", 0, NC)
            H2K = [k for kt in range(KT_I)
                   for k in bk("hz", hzc(kt) + 3, hzc(kt) + 3 + NC)]
            for n in range(NT):
                sqk = []
                for kt in range(KT_I):
                    d0 = hzc(kt) + 3 + n * 512
                    f0 = kt * 512
                    pr.emit("act", [lambda a, d0=d0, f0=f0: a.activation(
                        fbuf[:, f0:f0 + 512], hz[:, d0:d0 + 512], AF.Square)],
                        reads=bk("hz", d0, d0 + 512), writes=bk("fbuf", f0, f0 + 512))
                    sqk += bk("fbuf", f0, f0 + 512)
                groups = [
                    (pbank(4 + n)[0:1, :],
                     [(ONES_COL, hz[:, hzc(kt) + 3 + n * 512:
                                  hzc(kt) + 3 + n * 512 + 512])
                      for kt in range(KT_I)]),
                    (pbank(n)[0:1, :],
                     [(ONES_COL, fbuf[:, kt * 512:(kt + 1) * 512])
                      for kt in range(KT_I)]),
                ]
                mm_emit(groups, list(ONES_KEYS) + H2K + sqk, pkeys(4 + n) + pkeys(n))
            pr.emit("dve", [lambda v: v.tensor_scalar(rows[0:1, 0:W],
                                                      pbank(4, 0, NT)[0:1, :],
                                                      1.0 / I2, None, ALU.mult)],
                    reads=pkeys(4, NT), writes=RK)
            pr.emit("dve", [lambda v: v.tensor_scalar(rows[0:1, NC:NC + W],
                                                      pbank(0, 0, NT)[0:1, :],
                                                      1.0 / I2, None, ALU.mult)],
                    reads=pkeys(0, NT), writes=RK)
            e3, e3k = escr_slot()
            pr.emit("dve", [lambda v, e3=e3: v.tensor_tensor(
                e3[0:1, 0:W], rows[0:1, 0:W], rows[0:1, 0:W], ALU.mult)],
                reads=RK, writes=e3k)
            pr.emit("dve", [lambda v, e3=e3: v.tensor_tensor(
                rows[0:1, NC:NC + W], rows[0:1, NC:NC + W], e3[0:1, 0:W],
                ALU.subtract)],
                reads=RK + e3k, writes=RK)
            pr.emit("act", [lambda a: a.activation(rows[0:1, NC:NC + W],
                                                   rows[0:1, NC:NC + W], AF.Sqrt,
                                                   bias=bias_f[0:1, 648:649],
                                                   scale=1.0)],
                    reads=RK + EPSK, writes=RK)
            pr.emit("dve", [lambda v: v.reciprocal(rows[0:1, NC:NC + W],
                                                   rows[0:1, NC:NC + W])],
                    reads=RK, writes=RK)
            pr.emit("dve", [lambda v: v.tensor_tensor(rows[0:1, 0:W],
                                                      rows[0:1, 0:W],
                                                      rows[0:1, NC:NC + W],
                                                      ALU.mult)],
                    reads=RK, writes=RK)
            pr.emit("dve", [lambda v: v.tensor_copy(rows_bf[0:1, 0:W],
                                                    rows[0:1, NC:NC + W])],
                    reads=RK, writes=RBK)
            groups = [(pbank(2 + n),
                       [(ONES_ROW128, rows_bf[0:1, n * 512:(n + 1) * 512])])
                      for n in range(NT)]
            mm_emit(groups, RBK + list(ONES_KEYS), pkeys(2, NT))
            pr.emit("dve", [lambda v: v.tensor_copy(rmb[:, 0:W], pbank(2, 0, NT))],
                    reads=pkeys(2, NT), writes=bk("rmb", 0, NC))
            pr.emit("dve", [lambda v: v.tensor_copy(rows_bf[0:1, 0:W],
                                                    rows[0:1, 0:W])],
                    reads=RK, writes=RBK)
            groups = [(pbank(2 + n),
                       [(ONES_ROW128, rows_bf[0:1, n * 512:(n + 1) * 512])])
                      for n in range(NT)]
            mm_emit(groups, RBK + list(ONES_KEYS), pkeys(2, NT))
            pr.emit("dve", [lambda v: v.tensor_copy(rmb[:, NC:NC + W],
                                                    pbank(2, 0, NT))],
                    reads=pkeys(2, NT), writes=bk("rmb", NC, 2 * NC))
            RMK = bk("rmb", 0, NC)
            MMK = bk("rmb", NC, 2 * NC)
            for kt in range(KT_I):
                z0 = hzc(kt) + 3
                d0 = kt * NC
                e1, e1k = escr_slot()
                e2, e2k = escr_slot()
                ga, gak = bias_ap(LNGC(kt))
                ba, bak = bias_ap(LNBC(kt))
                pr.emit("pool", [lambda v, z0=z0, e1=e1: v.tensor_tensor(
                    e1, hz[:, z0:z0 + W], rmb[:, 0:W], ALU.mult)],
                    reads=bk("hz", z0, z0 + W) + RMK, writes=e1k)
                pr.emit("pool", [lambda v, e1=e1, e2=e2: v.tensor_tensor(
                    e2, e1, rmb[:, NC:NC + W], ALU.subtract)],
                    reads=e1k + MMK, writes=e2k)
                pr.emit("dve", [lambda v, e2=e2, d0=d0, ga=ga, ba=ba: v.tensor_scalar(
                    cbuf[:, d0:d0 + W], e2, ga, ba, ALU.mult, ALU.add)],
                    reads=e2k + gak + bak, writes=bk("cbuf", d0, d0 + W))
            if tap_here:
                tap("hn", cbuf[:, :], 128, KT_I * NC, BF16, bk("cbuf", 0, KT_I * NC))

            if c == 0:
                prep_layer(w_cproj, I2, H, scr_cproj, "scr_cproj", 256)

            CBK_ALL = [k for kt in range(KT_I)
                       for k in bk("cbuf", kt * NC, kt * NC + NC)]
            for s in range(min(NS, KT_H)):
                load_slab("sp", s % NS, 2048, scr_fc[s, :, :], [("scr_fc", s)],
                          chan=f"a{s % NS}")
            for m in range(KT_H):
                sc0 = (m % NS) * 2048
                bp = (m % 3) * 2
                groups = [(pbank(bp + n),
                           [(gw[:, sc0 + kt * 128:sc0 + (kt + 1) * 128],
                             cbuf[:, kt * NC + n * 512:kt * NC + n * 512 + 512])
                            for kt in range(KT_I)]) for n in range(NT)]
                mm_emit(groups, bk("gw", sc0, sc0 + 2048) + CBK_ALL, pkeys(bp, NT))
                fa, fak = bias_ap(FCBC(m))
                d0 = m * NC
                pr.emit("act", [lambda a, bp=bp, d0=d0, fa=fa: a.activation(
                    fbuf[:, d0:d0 + W], pbank(bp, 0, NT), AF.Silu, bias=fa,
                    scale=1.0)],
                    reads=pkeys(bp, NT) + fak, writes=bk("fbuf", d0, d0 + W))
                if m + NS < KT_H:
                    load_slab("sp", (m + NS) % NS, 2048, scr_fc[m + NS, :, :],
                              [("scr_fc", m + NS)], chan=f"a{(m + NS) % NS}")
            if tap_here:
                tap("u", fbuf[:, 0:KT_H * NC], 128, KT_H * NC, BF16,
                    bk("fbuf", 0, KT_H * NC))

            if c == 0:
                prep_layer(w_out, H, I2, scr_outw, "scr_outw", 128)

            UK_ALL = [k for kt in range(KT_H)
                      for k in bk("fbuf", kt * NC, kt * NC + NC)]
            for s in range(min(NS, 8)):
                load_slab("sp", s % NS, 2048, scr_cproj[s, :, :],
                          [("scr_cproj", s)], chan=f"a{s % NS}")
            for s in range(8):
                sc0 = (s % NS) * 2048
                for mmi in range(2):
                    m = 2 * s + mmi
                    bp = (m % 3) * 2
                    groups = [(pbank(bp + n),
                               [(gw[:, sc0 + kt * 256 + mmi * 128:
                                    sc0 + kt * 256 + mmi * 128 + 128],
                                 fbuf[:, kt * NC + n * 512:kt * NC + n * 512 + 512])
                                for kt in range(KT_H)]) for n in range(NT)]
                    mm_emit(groups, bk("gw", sc0, sc0 + 2048) + UK_ALL, pkeys(bp, NT))
                    ca, cak = bias_ap(CPROJBC(m))
                    d0 = m * NC
                    z0 = hzc(m) + 3
                    pr.emit("dve", [lambda v, bp=bp, d0=d0, z0=z0, ca=ca:
                                    v.scalar_tensor_tensor(
                        cbuf[:, d0:d0 + W], hz[:, z0:z0 + W], ca,
                        pbank(bp, 0, NT), ALU.add, ALU.add)],
                        reads=pkeys(bp, NT) + bk("hz", z0, z0 + W) + cak,
                        writes=bk("cbuf", d0, d0 + W))
                if s + NS < 8:
                    load_slab("sp", (s + NS) % NS, 2048, scr_cproj[s + NS, :, :],
                              [("scr_cproj", s + NS)], chan=f"a{(s + NS) % NS}")
            if tap_here:
                tap("hf", cbuf[:, :], 128, KT_I * NC, BF16, bk("cbuf", 0, KT_I * NC))

            for s in range(min(NS, KT_H)):
                load_slab("sp", s % NS, 2048, scr_outw[s, :, :], [("scr_outw", s)],
                          chan=f"a{s % NS}")
            for m in range(KT_H):
                sc0 = (m % NS) * 2048
                bp = (m % 3) * 2
                groups = [(pbank(bp + n),
                           [(gw[:, sc0 + kt * 128:sc0 + (kt + 1) * 128],
                             cbuf[:, kt * NC + n * 512:kt * NC + n * 512 + 512])
                            for kt in range(KT_I)]) for n in range(NT)]
                mm_emit(groups, bk("gw", sc0, sc0 + 2048) + CBK_ALL, pkeys(bp, NT))
                d0 = m * NC
                pr.emit("dve", [lambda v, bp=bp, d0=d0: v.tensor_copy(
                    sb_s[:, d0:d0 + W], pbank(bp, 0, NT))],
                    reads=pkeys(bp, NT), writes=bk("sb_s", d0, d0 + W))
                if m + NS < KT_H:
                    load_slab("sp", (m + NS) % NS, 2048, scr_outw[m + NS, :, :],
                              [("scr_outw", m + NS)], chan=f"a{(m + NS) % NS}")
            if tap_here:
                tap("s", sb_s[:, :], 128, KT_H * NC, BF16, bk("sb_s", 0, KT_H * NC))

            for e in range(EXP):
                bp = (e % 2) * 2
                groups = [(pbank(bp + n),
                           [(selb[0:8, e * 128:(e + 1) * 128],
                             comb8[0:8, n * 512:(n + 1) * 512])]) for n in range(NT)]
                mm_emit(groups,
                        bk("comb8", 0, NC) + bk("selb", e * 128, (e + 1) * 128),
                        pkeys(bp, NT))
                r0 = e * NC
                pr.emit("dve", [lambda v, bp=bp, r0=r0: v.tensor_copy(
                    fbuf[:, r0:r0 + W], pbank(bp, 0, NT))],
                    reads=pkeys(bp, NT), writes=bk("fbuf", r0, r0 + W))
            if tap_here:
                tap("gout", gout_fm[:, :], 8, NC, BF16, bk("gout_fm", 0, NC))
                tap("cb", fbuf[:, 0:EXP * NC], 128, EXP * NC, BF16,
                    bk("fbuf", 0, EXP * NC))

            idx12 = lambda e, wi, s: (e * 2 + wi) * 8 + s
            idxgl = lambda e, wi, m: (e * 2 + wi) * 16 + m
            idx3 = lambda e, m: e * 8 + m
            ZK_ALL = [k for kt in range(KT_I)
                      for k in bk("hz", hzc(kt) + 3, hzc(kt) + 3 + NC)]
            for e in range(EXP):
                def get12(slot, wi, sidx, e=e, c=c):
                    if c == 0:
                        src = (w1 if wi == 0 else w2)[e, :,
                                                      sidx * 256:(sidx + 1) * 256] \
                            .rearrange("(kt p) m -> p kt m", p=128)
                        convert_into_slot(slot, src, scr_w12, "scr_w12",
                                          idx12(e, wi, sidx))
                    else:
                        load_slab("sp", slot, 2048,
                                  scr_w12[idx12(e, wi, sidx), :, :],
                                  [("scr_w12", idx12(e, wi, sidx))])

                def getgl(slot, wi, midx, e=e, c=c):
                    if c == 0:
                        src = (wg if wi == 0 else wl)[e, :,
                                                      midx * 128:(midx + 1) * 128] \
                            .rearrange("(kt p) m -> p kt m", p=128)
                        convert_into_slot(slot, src, scr_wgl, "scr_wgl",
                                          idxgl(e, wi, midx))
                    else:
                        load_slab("sp", slot, 2048,
                                  scr_wgl[idxgl(e, wi, midx), :, :],
                                  [("scr_wgl", idxgl(e, wi, midx))])

                def get3(slot, midx, e=e, c=c):
                    if c == 0:
                        src = w3[e, :, midx * 128:(midx + 1) * 128] \
                            .rearrange("(kt p) m -> p kt m", p=128)
                        convert_into_slot(slot, src, scr_w3, "scr_w3", idx3(e, midx))
                    else:
                        load_slab("sp", slot, 2048, scr_w3[idx3(e, midx), :, :],
                                  [("scr_w3", idx3(e, midx))])

                for s in range(2):
                    get12(s % 2, 0, s)
                    get12(2 + s % 2, 1, s)
                for s in range(8):
                    scA = (s % 2) * 2048
                    scB = (2 + s % 2) * 2048
                    for mmi in range(2):
                        m = 2 * s + mmi
                        bpA = 0 if m % 2 == 0 else 4
                        groups = []
                        for n in range(NT):
                            groups.append((pbank(bpA + n),
                                           [(gw[:, scA + kt * 256 + mmi * 128:
                                                scA + kt * 256 + mmi * 128 + 128],
                                             xn_ap(kt, n)) for kt in range(KT_H)]))
                        for n in range(NT):
                            groups.append((pbank(2 + n),
                                           [(gw[:, scB + kt * 256 + mmi * 128:
                                                scB + kt * 256 + mmi * 128 + 128],
                                             xn_ap(kt, n)) for kt in range(KT_H)]))
                        rd = bk("gw", scA, scA + 2048) + bk("gw", scB, scB + 2048) \
                            + XNK_ALL
                        mm_emit(groups, rd, pkeys(bpA, NT) + pkeys(2, NT))
                        e1, e1k = escr_slot()
                        e2, e2k = escr_slot()
                        b1a, b1k = bias_ap(B1C(e, m))
                        b2a, b2k = bias_ap(B2C(e, m))
                        pr.emit("dve", [lambda v, bpA=bpA, e1=e1, b1a=b1a:
                                        v.tensor_scalar(e1, pbank(bpA, 0, NT), b1a,
                                                        None, ALU.add)],
                                reads=pkeys(bpA, NT) + b1k, writes=e1k)
                        pr.emit("dve", [lambda v, e2=e2, b2a=b2a:
                                        v.tensor_scalar(e2, pbank(2, 0, NT), b2a,
                                                        None, ALU.add)],
                                reads=pkeys(2, NT) + b2k, writes=e2k)
                        d0 = hzc(m) + 3
                        pr.emit("dve", [lambda v, e1=e1, e2=e2, d0=d0:
                                        v.tensor_tensor(hz[:, d0:d0 + W], e1, e2,
                                                        ALU.mult)],
                                reads=e1k + e2k, writes=bk("hz", d0, d0 + W))
                    if s + 2 < 8:
                        get12((s + 2) % 2, 0, s + 2)
                        get12(2 + (s + 2) % 2, 1, s + 2)
                if tap_here and e == tap_expert:
                    tap("z", hz[:, :], 128, KT_I * (NC + 3), BF16, HZK_ALL)

                for s in range(2):
                    getgl(s % 2, 0, s)
                    getgl(2 + s % 2, 1, s)
                for m in range(KT_I):
                    scG = (m % 2) * 2048
                    scL = (2 + m % 2) * 2048
                    bpG = 0 if m % 2 == 0 else 4
                    groups = []
                    for n in range(NT):
                        groups.append((pbank(bpG + n),
                                       [(gw[:, scG + kt * 128:scG + (kt + 1) * 128],
                                         hz[:, hzc(kt) + 3 + n * 512:
                                            hzc(kt) + 3 + n * 512 + 512])
                                        for kt in range(KT_I)]))
                    for n in range(NT):
                        groups.append((pbank(2 + n),
                                       [(gw[:, scL + kt * 128:scL + (kt + 1) * 128],
                                         hz[:, hzc(kt) + 3 + n * 512:
                                            hzc(kt) + 3 + n * 512 + 512])
                                        for kt in range(KT_I)]))
                    rd = bk("gw", scG, scG + 2048) + bk("gw", scL, scL + 2048) + ZK_ALL
                    mm_emit(groups, rd, pkeys(bpG, NT) + pkeys(2, NT))
                    sg, sgk = escr_slot()
                    tl, tlk = escr_slot()
                    bga, bgk = bias_ap(BGC(e, m))
                    bla, blk = bias_ap(BLC(e, m))
                    pr.emit("act", [lambda a, bpG=bpG, sg=sg, bga=bga: a.activation(
                        sg, pbank(bpG, 0, NT), AF.Silu, bias=bga, scale=1.0)],
                        reads=pkeys(bpG, NT) + bgk, writes=sgk)
                    pr.emit("dve", [lambda v, tl=tl, bla=bla: v.tensor_scalar(
                        tl, pbank(2, 0, NT), bla, None, ALU.add)],
                        reads=pkeys(2, NT) + blk, writes=tlk)
                    d0 = m * NC
                    pr.emit("dve", [lambda v, sg=sg, tl=tl, d0=d0: v.tensor_tensor(
                        cbuf[:, d0:d0 + W], sg, tl, ALU.mult)],
                        reads=sgk + tlk, writes=bk("cbuf", d0, d0 + W))
                    if m + 2 < KT_I:
                        getgl((m + 2) % 2, 0, m + 2)
                        getgl(2 + (m + 2) % 2, 1, m + 2)
                if tap_here and e == tap_expert:
                    tap("o", cbuf[:, :], 128, KT_I * NC, BF16,
                        bk("cbuf", 0, KT_I * NC))

                for s in range(2):
                    get3(s % 2, s)
                for m in range(KT_H):
                    sc0 = (m % 2) * 2048
                    bp = (m % 3) * 2
                    groups = [(pbank(bp + n),
                               [(gw[:, sc0 + kt * 128:sc0 + (kt + 1) * 128],
                                 cbuf[:, kt * NC + n * 512:kt * NC + n * 512 + 512])
                                for kt in range(KT_I)]) for n in range(NT)]
                    mm_emit(groups, bk("gw", sc0, sc0 + 2048) + CBK_ALL,
                            pkeys(bp, NT))
                    ty, tyk = escr_slot()
                    b3a, b3k = bias_ap(B3C(e, m))
                    pr.emit("dve", [lambda v, bp=bp, ty=ty, b3a=b3a: v.tensor_scalar(
                        ty, pbank(bp, 0, NT), b3a, None, ALU.add)],
                        reads=pkeys(bp, NT) + b3k, writes=tyk)
                    cb0 = e * NC
                    cbk2 = bk("fbuf", cb0, cb0 + W)
                    d0 = m * NC
                    ybk = bk("yb", d0, d0 + W)
                    if e == 0:
                        pr.emit("dve", [lambda v, ty=ty, cb0=cb0, d0=d0:
                                        v.tensor_tensor(yb[:, d0:d0 + W], ty,
                                                        fbuf[:, cb0:cb0 + W],
                                                        ALU.mult)],
                                reads=tyk + cbk2, writes=ybk)
                    else:
                        t2, t2k = escr_slot()
                        pr.emit("dve", [lambda v, ty=ty, cb0=cb0, t2=t2:
                                        v.tensor_tensor(t2, ty, fbuf[:, cb0:cb0 + W],
                                                        ALU.mult)],
                                reads=tyk + cbk2, writes=t2k)
                        pr.emit("dve", [lambda v, t2=t2, d0=d0: v.tensor_tensor(
                            yb[:, d0:d0 + W], yb[:, d0:d0 + W], t2, ALU.add)],
                            reads=t2k + ybk, writes=ybk)
                    if m + 2 < KT_H:
                        get3((m + 2) % 2, m + 2)
            if tap_here:
                tap("y", yb[:, :], 128, KT_H * NC, BF16, bk("yb", 0, KT_H * NC))

            for kt in range(KT_H):
                d0 = kt * NC
                e1, e1k = escr_slot()
                pr.emit("act", [lambda a, d0=d0, e1=e1: a.activation(
                    e1, yb[:, d0:d0 + W], AF.Silu)],
                    reads=bk("yb", d0, d0 + W), writes=e1k)
                pr.emit("pool", [lambda v, e1=e1, d0=d0: v.tensor_tensor(
                    cbuf[:, d0:d0 + W], e1, sb_s[:, d0:d0 + W], ALU.mult)],
                    reads=e1k + bk("sb_s", d0, d0 + W),
                    writes=bk("cbuf", d0, d0 + W))
            for tt in range(TT):
                tb = tt % 2
                fk = []
                trs = []
                for kt in range(KT_H):
                    c0 = kt * NC + tt * 128
                    trs.append((ptb(tb)[:, kt * 128:(kt + 1) * 128],
                                cbuf[:, c0:c0 + 128]))
                    fk += bk("cbuf", c0, c0 + 128)
                tr_group(tb, trs, fk)
                xs = tt % 2
                xsl = xt[:, xs * H:(xs + 1) * H]
                xkeys = bk("xt", xs * H, (xs + 1) * H)
                pr.dma("act", f"x{xs}", lambda h, r0=t0 + tt * 128, d=xsl:
                       h.dma_start(out=d, in_=x[r0:r0 + 128, :]), writes=xkeys)
                osl = ot[:, xs * H:(xs + 1) * H]
                okeys = bk("ot", xs * H, (xs + 1) * H)
                pr.emit("dve", [lambda v, tb=tb, osl=osl, xsl=xsl: v.tensor_tensor(
                    osl, ptb(tb)[:, 0:1024], xsl, ALU.add)],
                    reads=ptk(tb) + xkeys, writes=okeys)
                pr.dma("act", f"o{xs}", lambda h, r0=t0 + tt * 128, sl=osl:
                       h.dma_start(out=out[r0:r0 + 128, :], in_=sl),
                       reads=okeys, writes=[("OUT", (t0 + tt * 128) // 512)])
                out_keys_all.append(("OUT", (t0 + tt * 128) // 512))

        def _final(h):
            for q, n in sorted(pr.qcnt.items()):
                h.wait_ge(pr.sems[q], n * 16)
            return h.nop()
        pr.emit("sp", [_final], reads=list(set(out_keys_all)))
        pr.flush(nc, ctx)
    return nc


def _core_inputs(inputs, core, T=4096):
    x = inputs["L0_attn_hidden_states"]
    B, L, Hm = x.shape
    b = (core * T) // L
    s0 = (core * T) % L
    xs = np.ascontiguousarray(x[b, s0:s0 + T, :])
    if s0 == 0:
        xhh = np.zeros((3, Hm), dtype=np.float32)
    else:
        xhh = np.ascontiguousarray(x[b, s0 - 3:s0, :])
    m = {"x": xs, "xh": xhh}
    for k, v in inputs.items():
        if k == "L0_attn_hidden_states":
            continue
        m[k] = np.ascontiguousarray(v)
    m["onesf"] = np.ones((128, 512), dtype=np.float32)
    m["identf"] = np.eye(128, dtype=np.float32)
    sel = np.zeros((8, 1024), dtype=np.float32)
    for e in range(8):
        sel[e, e * 128:(e + 1) * 128] = 1.0
    m["sel8"] = sel
    return m


def kernel(**inputs):
    x = inputs["L0_attn_hidden_states"]
    B, L, Hm = x.shape
    n_cores = 8
    T = (B * L) // n_cores
    nc = build_program(T=T, NC=1024)
    in_maps = [_core_inputs(inputs, c, T) for c in range(n_cores)]
    res = run_bass_kernel_spmd(nc, in_maps, core_ids=list(range(n_cores)))
    outs = [res.results[c]["out"] for c in range(n_cores)]
    return np.concatenate(outs, axis=0).reshape(B, L, Hm)


# version 14 (final)
# speedup vs baseline: 1.1221x; 1.1221x over previous
import numpy as np
import ml_dtypes
from contextlib import ExitStack

import concourse.bass as bass
import concourse.mybir as mybir
from concourse.bass_utils import run_bass_kernel_spmd

F32 = mybir.dt.float32
BF16 = mybir.dt.bfloat16
AF = mybir.ActivationFunctionType
ALU = mybir.AluOpType
AX = mybir.AxisListType

H = 1024
I2 = 2048
EXP = 8
FF = 2048
KCONV = 4
NGRP = 8
KT_H = H // 128
KT_I = I2 // 128
EPS = 1e-5

ENGS = ("pe", "act", "dve", "sp", "pool")


class Prog:

    def __init__(self):
        self.items = {e: [] for e in ENGS}
        self.cnt = dict.fromkeys(ENGS, 0)
        self.qcnt = {}
        self.res = {}
        self.clock = {s: {} for s in ENGS}
        self.opclock = {}
        self.sems = {}
        self.chan_warn = set()
        self.nwaits = dict.fromkeys(ENGS, 0)
        self.nfns = dict.fromkeys(ENGS, 0)

    @staticmethod
    def keys(name, c0, c1):
        return [(name, b) for b in range(c0 // 512, (c1 + 511) // 512)]

    def _deps(self, reads, writes):
        deps = []
        for k in reads:
            ent = self.res.get(k)
            if ent and ent[0] is not None:
                deps.append(ent[0])
        for k in writes:
            ent = self.res.get(k)
            if ent:
                if ent[0] is not None:
                    deps.append(ent[0])
                deps.extend(ent[1])
        return deps

    def _need(self, eng, deps):
        need = {}
        clk = self.clock[eng]
        for (s, i) in deps:
            if clk.get(s, 0) >= i:
                continue
            if need.get(s, 0) < i:
                need[s] = i
        return need

    def _merge(self, eng, need):
        clk = self.clock[eng]
        for s, i in need.items():
            oc = self.opclock.get((s, i), {})
            for s2, i2 in oc.items():
                if clk.get(s2, 0) < i2:
                    clk[s2] = i2
            if clk.get(s, 0) < i:
                clk[s] = i

    def _update_res(self, op, reads, writes):
        wset = set(writes)
        for k in reads:
            if k in wset:
                continue
            ent = self.res.get(k)
            if ent is None:
                ent = (None, [])
                self.res[k] = ent
            ent[1].append(op)
        for k in writes:
            self.res[k] = (op, [])

    def emit(self, eng, fns, reads=(), writes=()):
        if not isinstance(fns, (list, tuple)):
            fns = [fns]
        need = self._need(eng, self._deps(reads, writes))
        self._merge(eng, need)
        self.cnt[eng] += 1
        idx = self.cnt[eng]
        op = (eng, idx)
        oc = dict(self.clock[eng])
        oc[eng] = idx
        self.opclock[op] = oc
        self.nwaits[eng] += len(need)
        self.nfns[eng] += len(fns)
        prog = self

        def closure(h, need=need, fns=fns, eng=eng):
            for s, i in sorted(need.items()):
                h.wait_ge(prog.sems[s], i * (16 if s.startswith("q_") else 1))
            last = None
            for f in fns:
                last = f(h)
            last.then_inc(prog.sems[eng], 1)
        self.items[eng].append(closure)
        self._update_res(op, reads, writes)
        return op

    def dma(self, eng, chan, fn, reads=(), writes=()):
        q = "q_" + chan
        if q not in self.qcnt:
            self.qcnt[q] = 0
            self.clock[q] = {}
        need = self._need(eng, self._deps(reads, writes))
        self._merge(eng, need)
        prev = self.qcnt[q]
        if prev > 0 and self.clock[eng].get(q, 0) < prev and q not in self.chan_warn:
            self.chan_warn.add(q)
            print(f"WARNING: dma channel {chan} may have >1 outstanding at "
                  f"issue #{prev + 1}")
        self.qcnt[q] += 1
        qidx = self.qcnt[q]
        qop = (q, qidx)
        oc = dict(self.clock[eng])
        oc[q] = qidx
        self.opclock[qop] = oc
        prog = self

        def closure(h, need=need, fn=fn, q=q):
            for s, i in sorted(need.items()):
                h.wait_ge(prog.sems[s], i * (16 if s.startswith("q_") else 1))
            fn(h).then_inc(prog.sems[q], 16)
        self.items[eng].append(closure)
        self._update_res(qop, reads, writes)
        return qop

    def flush(self, nc, ctx):
        for s in list(ENGS) + list(self.qcnt.keys()):
            self.sems[s] = ctx.enter_context(nc.semaphore(f"sem_{s}"))
        block = ctx.enter_context(nc.Block())
        for eng, attr in (("pe", "tensor"), ("act", "scalar"), ("dve", "vector"),
                          ("sp", "sync"), ("pool", "gpsimd")):
            items = self.items[eng]
            if items:
                getattr(block, attr)(lambda h, items=items: [f(h) for f in items])


def build_program(T=4096, NC=1024, taps=(), tap_chunk=0, tap_expert=0):
    NT = NC // 512
    TT = NC // 128
    CH = T // NC
    W = NT * 512
    assert NC % 512 == 0 and T % NC == 0 and NT in (1, 2)

    nc = bass.Bass("TRN2", target_bir_lowering=False)

    def din(name, shape, dt=F32):
        return nc.dram_tensor(name, shape, dt, kind="ExternalInput").ap()

    x = din("x", [T, H])
    xh = din("xh", [3, H])
    w_inproj = din("L0_attn_in_proj_w", [2 * I2, H])
    w_conv = din("L0_attn_conv_w", [I2, I2 // NGRP, KCONV])
    b_conv = din("L0_attn_conv_b", [I2])
    ln_g = din("L0_attn_ln_g", [I2])
    ln_b = din("L0_attn_ln_b", [I2])
    w_fc = din("L0_attn_fc_w", [H, I2])
    b_fc = din("L0_attn_fc_b", [H])
    w_cproj = din("L0_attn_cproj_w", [I2, H])
    b_cproj = din("L0_attn_cproj_b", [I2])
    w_out = din("L0_attn_out_w", [H, I2])
    w_gate = din("L0_ffn_Wgate", [H, EXP])
    b_gate = din("L0_ffn_bgate", [EXP])
    w1 = din("L0_ffn_W1", [EXP, H, FF])
    b1 = din("L0_ffn_b1", [EXP, FF])
    w2 = din("L0_ffn_W2", [EXP, H, FF])
    b2 = din("L0_ffn_b2", [EXP, FF])
    wg = din("L0_ffn_Wg", [EXP, FF, FF])
    bg = din("L0_ffn_bg", [EXP, FF])
    wl = din("L0_ffn_Wl", [EXP, FF, FF])
    bl = din("L0_ffn_bl", [EXP, FF])
    w3 = din("L0_ffn_W3", [EXP, FF, H])
    b3 = din("L0_ffn_b3", [EXP, H])
    onesf = din("onesf", [128, 512])
    identf_in = din("identf", [128, 128])
    sel8_in = din("sel8", [8, 1024])

    out = nc.dram_tensor("out", [T, H], F32, kind="ExternalOutput").ap()

    scr_inproj = nc.dram_tensor("scr_inproj", [16, 128, KT_H * 256], BF16).ap()
    scr_fc = nc.dram_tensor("scr_fc", [8, 128, KT_I * 128], BF16).ap()
    scr_cproj = nc.dram_tensor("scr_cproj", [8, 128, KT_H * 256], BF16).ap()
    scr_outw = nc.dram_tensor("scr_outw", [8, 128, KT_I * 128], BF16).ap()
    scr_conv = nc.dram_tensor("scr_conv", [16, 128, 8 * 128], BF16).ap()
    scr_w12 = nc.dram_tensor("scr_w12", [EXP * 2 * 8, 128, 2048], BF16).ap()
    scr_wgl = nc.dram_tensor("scr_wgl", [EXP * 2 * 16, 128, 2048], BF16).ap()
    scr_w3 = nc.dram_tensor("scr_w3", [EXP * 8, 128, 2048], BF16).ap()

    tap_outs = {}

    ctx = ExitStack()
    with ctx:
        def sbt(name, shape, dt):
            return ctx.enter_context(nc.sbuf_tensor(name, shape, dt))

        xnb = sbt("xnb", [128, KT_H * NC], BF16)
        hz = sbt("hz", [128, KT_I * (NC + 3)], BF16)
        cbuf = sbt("cbuf", [128, KT_I * NC], BF16)
        fbuf = sbt("fbuf", [128, max(KT_I * 512, EXP * NC, KT_H * NC)], BF16)
        sb_s = sbt("sb_s", [128, KT_H * NC], BF16)
        yb = sbt("yb", [128, KT_H * NC], BF16)
        gw = sbt("gw", [128, 4 * 2048], BF16)
        xt = sbt("xt", [128, 2 * H], F32)
        ot = sbt("ot", [128, 2 * H], F32)
        xtm = sbt("xtm", [128, 2 * H], BF16)
        escr = sbt("escr", [128, 3 * 1024], BF16)
        bias_f = sbt("bias_f", [128, 652], F32)
        rmb = sbt("rmb", [128, 2 * NC], BF16)
        rows = sbt("rows", [1, 2 * NC], F32)
        rows_bf = sbt("rows_bf", [1, NC], BF16)
        gout_fm = sbt("gout_fm", [8, NC], BF16)
        gtw = sbt("gtw", [128, 8 * TT], F32)
        gops = sbt("gops", [128, 32], F32)
        gsc = sbt("gsc", [128, 8], F32)
        combtm = sbt("combtm", [128, 8], BF16)
        comb8 = sbt("comb8", [8, NC], BF16)
        selb = sbt("selb", [8, 1024], BF16)
        wgres = sbt("wgres", [128, KT_H * 8], BF16)
        bgate_row = sbt("bgate_row", [1, 8], BF16)
        convbrow = sbt("convbrow", [1, I2], BF16)
        onesb = sbt("onesb", [128, 512], BF16)
        identb = sbt("identb", [128, 128], BF16)
        identf = sbt("identf_s", [128, 128], F32)
        stat = sbt("stat", [128, 16], F32)
        xnh_T = sbt("xnh_T", [128, 24], BF16)
        htail = sbt("htail", [128, KT_I * 3], BF16)
        bwork = sbt("bwork", [128, 128], F32)
        pland = sbt("pland", [128, 2048], F32)

        pbig = ctx.enter_context(nc.psum_tensor("pbig", [128, 6 * 512], F32))
        pTT = ctx.enter_context(nc.psum_tensor("pTT", [128, 2 * 1024], BF16))

        def pbank(b, n0=0, nn=1):
            return pbig[:, (b + n0) * 512:(b + n0 + nn) * 512]

        def pkeys(b, nn=1):
            return [("pb", b + i) for i in range(nn)]

        def ptb(t):
            return pTT[:, t * 1024:(t + 1) * 1024]

        def ptk(t):
            return [("pT", t)]

        pr = Prog()
        bk = Prog.keys

        esl = [0]

        def escr_slot():
            s = esl[0] % 3
            esl[0] += 1
            return escr[:, s * 1024:s * 1024 + W], bk("escr", s * 1024, s * 1024 + W)

        pr.dma("pool", "ini0", lambda h: h.dma_start(out=onesb[:, :], in_=onesf[:, :]),
               writes=bk("onesb", 0, 512))
        pr.dma("pool", "ini1", lambda h: h.dma_start(out=identb[:, :], in_=identf_in[:, :]),
               writes=bk("identb", 0, 128))
        pr.dma("sp", "ini2", lambda h: h.dma_start(out=identf[:, :], in_=identf_in[:, :]),
               writes=bk("identf_s", 0, 128))
        ONES_COL = onesb[:, 0:1]
        ONES_ROW128 = onesb[0:1, 0:128]
        ONES_ROW512 = onesb[0:1, 0:512]
        ONES_KEYS = bk("onesb", 0, 512)
        IDB_KEYS = bk("identb", 0, 128)
        IDF_KEYS = bk("identf_s", 0, 128)

        pr.dma("pool", "ini3", lambda h: h.dma_start(
            out=wgres[:, :].rearrange("p (kt m) -> p kt m", kt=KT_H),
            in_=w_gate[:, :].rearrange("(kt p) m -> p kt m", p=128)),
            writes=bk("wgres", 0, KT_H * 8))
        pr.dma("pool", "ini4", lambda h: h.dma_start(out=bgate_row[:, :],
                                                     in_=b_gate[None, :]),
               writes=bk("bgate_row", 0, 8))
        pr.dma("pool", "ini5", lambda h: h.dma_start(out=selb[:, :], in_=sel8_in[:, :]),
               writes=bk("selb", 0, 1024))
        pr.dma("pool", "ini6", lambda h: h.dma_start(
            out=convbrow[:, :], in_=b_conv[None, :]), writes=bk("convbrow", 0, I2))
        pr.emit("dve", [lambda v: v.memset(bias_f[:, 648:649], EPS)],
                writes=bk("bias_f", 648, 649))

        def mm_emit(pairs_banks, reads, wkeys):
            assert len(pairs_banks) == len(wkeys)
            for (out_ap, pairs), wk in zip(pairs_banks, wkeys):
                npair = len(pairs)
                fns = [lambda t, l=l, r=r, i=i, n=npair, o=out_ap:
                       t.matmul(o, l, r, start=(i == 0), stop=(i == n - 1))
                       for i, (l, r) in enumerate(pairs)]
                pr.emit("pe", fns, reads=reads, writes=[wk])

        def tr_group(tb, triples, reads, np_=128):
            fns = [lambda t, o=o, s=s, np_=np_: t.transpose(o, s, identb[0:np_, 0:np_])
                   for (o, s) in triples]
            return pr.emit("pe", fns, reads=list(reads) + IDB_KEYS, writes=ptk(tb))

        def load_slab(qeng, slot, ncols, src_ap, src_keys=(), chan=None):
            c0 = slot * 2048
            ch = chan or f"w{slot}"
            pr.dma(qeng, ch,
                   lambda h, d=gw[:, c0:c0 + ncols], s=src_ap: h.dma_start(out=d, in_=s),
                   reads=src_keys, writes=bk("gw", c0, c0 + ncols))
            return c0

        def land_slot(p, ncols):
            if p == 0:
                return pland[:, 0:ncols], bk("pland", 0, ncols)
            if p == 1:
                return xt[:, 0:ncols], bk("xt", 0, ncols)
            return ot[:, 0:ncols], bk("ot", 0, ncols)

        cvn = [0]

        def convert_into_slot(slot, src_f32_3d, scr, scr_name, idx, ncols=2048):
            p = cvn[0] % 3
            cvn[0] += 1
            land, lkeys = land_slot(p, ncols)
            pr.dma("act", f"cl{p}", lambda h, d=land, sap=src_f32_3d:
                   h.dma_start(out=d, in_=sap), writes=lkeys)
            c0 = slot * 2048
            gk = bk("gw", c0, c0 + ncols)
            pr.emit("dve", [lambda v, c0=c0, ncols=ncols, land=land:
                            v.tensor_copy(gw[:, c0:c0 + ncols], land)],
                    reads=lkeys, writes=gk)
            pr.dma("sp", f"cs{slot}", lambda h, sidx=idx, c0=c0, scr=scr, nc_=ncols:
                   h.dma_start(out=scr[sidx, :, 0:nc_], in_=gw[:, c0:c0 + nc_]),
                   reads=gk, writes=[(scr_name, idx)])

        def bias_pack_f32(specs, dst_c0, width):
            for (src, r0, nr) in specs:
                pr.dma("sp", "bias", lambda h, s=src, r0=r0, nr=nr:
                       h.dma_start(out=bwork[r0:r0 + nr, 0:128], in_=s),
                       writes=bk("bwork", 0, 128))
            pr.emit("pe", [lambda t: t.transpose(pbank(5)[:, 0:128], bwork[:, 0:128],
                                                 identf[:, :])],
                    reads=bk("bwork", 0, 128) + IDF_KEYS, writes=pkeys(5))
            pr.emit("dve", [lambda v, c0=dst_c0, w=width:
                            v.tensor_copy(bias_f[:, c0:c0 + w], pbank(5)[:, 0:w])],
                    reads=pkeys(5), writes=bk("bias_f", dst_c0, dst_c0 + width))

        for epair in range(4):
            e0 = 2 * epair
            specs = []
            r = 0
            for e in (e0, e0 + 1):
                for vec in (b1, b2, bg, bl):
                    specs.append((vec[e, :].rearrange("(q p) -> q p", p=128), r, 16))
                    r += 16
            bias_pack_f32(specs, e0 * 64, 128)
        bias_pack_f32([(b3[e, :].rearrange("(q p) -> q p", p=128), e * 8, 8)
                       for e in range(EXP)], 512, 64)
        bias_pack_f32([
            (b_conv[:].rearrange("(q p) -> q p", p=128), 0, 16),
            (ln_g[:].rearrange("(q p) -> q p", p=128), 16, 16),
            (ln_b[:].rearrange("(q p) -> q p", p=128), 32, 16),
            (b_cproj[:].rearrange("(q p) -> q p", p=128), 48, 16),
        ], 576, 64)
        bias_pack_f32([(b_fc[:].rearrange("(q p) -> q p", p=128), 0, 8)], 640, 8)

        def bias_ap(col):
            return bias_f[:, col:col + 1], bk("bias_f", col, col + 1)

        B1C = lambda e, m: e * 64 + m
        B2C = lambda e, m: e * 64 + 16 + m
        BGC = lambda e, m: e * 64 + 32 + m
        BLC = lambda e, m: e * 64 + 48 + m
        B3C = lambda e, m: 512 + e * 8 + m
        LNGC = lambda m: 592 + m
        LNBC = lambda m: 608 + m
        CPROJBC = lambda m: 624 + m
        FCBC = lambda m: 640 + m
        EPSK = bk("bias_f", 648, 649)

        def prep_layer(w_ap, OUT, IN, scr, scr_name, SW):
            KT = IN // 128
            MMs = SW // 128
            for s in range(OUT // SW):
                for mmi in range(MMs):
                    ob = s * MMs + mmi
                    p = cvn[0] % 3
                    cvn[0] += 1
                    land, lkeys = land_slot(p, IN)
                    pr.dma("act", f"cl{p}", lambda h, ob=ob, d=land, w_ap=w_ap:
                           h.dma_start(out=d, in_=w_ap[ob * 128:(ob + 1) * 128, :]),
                           writes=lkeys)
                    lc0 = 2 * 2048
                    pr.emit("dve", [lambda v, lc0=lc0, IN=IN, land=land:
                                    v.tensor_copy(gw[:, lc0:lc0 + IN], land)],
                            reads=lkeys, writes=bk("gw", lc0, lc0 + IN))
                    for kth in range(0, KT, 8):
                        kn = min(8, KT - kth)
                        tb = (kth // 8) % 2
                        tr_group(tb, [(ptb(tb)[:, i * 128:(i + 1) * 128],
                                       gw[:, lc0 + (kth + i) * 128:
                                          lc0 + (kth + i + 1) * 128])
                                      for i in range(kn)],
                                 bk("gw", lc0 + kth * 128, lc0 + (kth + kn) * 128))
                        sc = 3 * 2048 + kth * SW + mmi * 128
                        dst = gw[:, 3 * 2048:4 * 2048].rearrange(
                            "p (kt w) -> p kt w", w=SW)[:, kth:kth + kn,
                                                        mmi * 128:(mmi + 1) * 128]
                        pr.emit("dve", [lambda v, tb=tb, dst=dst, kn=kn:
                                        v.tensor_copy(dst, ptb(tb)[:, 0:kn * 128]
                                                      .rearrange("p (kt w) -> p kt w",
                                                                 w=128))],
                                reads=ptk(tb),
                                writes=bk("gw", 3 * 2048 + kth * SW,
                                          3 * 2048 + (kth + kn) * SW))
                pr.dma("sp", "scr", lambda h, s=s, KT=KT, SW=SW, scr=scr:
                       h.dma_start(out=scr[s, :, :],
                                   in_=gw[:, 3 * 2048:3 * 2048 + KT * SW]),
                       reads=bk("gw", 3 * 2048, 3 * 2048 + KT * SW),
                       writes=[(scr_name, s)])

        def prep_conv():
            for g in range(NGRP):
                for otile in range(2):
                    m = 2 * g + otile
                    lc0 = 2 * 2048
                    p = cvn[0] % 3
                    cvn[0] += 1
                    land, lkeys = land_slot(p, 1024)
                    pr.dma("act", f"cl{p}", lambda h, g=g, otile=otile, d=land:
                           h.dma_start(out=d,
                                       in_=w_conv[g * 256 + otile * 128:
                                                  g * 256 + (otile + 1) * 128, :, :]
                                       .rearrange("p ci k -> p (ci k)")),
                           writes=lkeys)
                    pr.emit("dve", [lambda v, lc0=lc0, land=land:
                                    v.tensor_copy(gw[:, lc0:lc0 + 1024], land)],
                            reads=lkeys, writes=bk("gw", lc0, lc0 + 1024))
                    for c2 in range(2):
                        tb = c2 % 2
                        tr_group(tb, [(ptb(tb)[:, tp * 128:(tp + 1) * 128],
                                       gw[:, lc0 + c2 * 512:lc0 + (c2 + 1) * 512]
                                       .rearrange("p (ci k) -> p k ci",
                                                  k=KCONV)[:, tp, :])
                                      for tp in range(KCONV)],
                                 bk("gw", lc0 + c2 * 512, lc0 + (c2 + 1) * 512))
                        j0 = c2 * KCONV
                        pr.emit("dve", [lambda v, tb=tb, j0=j0: v.tensor_copy(
                            gw[:, 3 * 2048 + j0 * 128:3 * 2048 + (j0 + 4) * 128],
                            ptb(tb)[:, 0:512])],
                            reads=ptk(tb),
                            writes=bk("gw", 3 * 2048 + j0 * 128,
                                      3 * 2048 + (j0 + 4) * 128))
                    pr.dma("sp", "scr", lambda h, m=m:
                           h.dma_start(out=scr_conv[m, :, :],
                                       in_=gw[:, 3 * 2048:3 * 2048 + 1024]),
                           reads=bk("gw", 3 * 2048, 3 * 2048 + 1024),
                           writes=[("scr_conv", m)])

        SK = bk("stat", 0, 16)
        pr.dma("sp", "x0", lambda h: h.dma_start(out=xt[0:3, 0:H], in_=xh[:, :]),
               writes=bk("xt", 0, H))

        def ln1_chain(rs, xsl, xkeys, osl, okeys, tmsl, tmkeys):
            pr.emit("dve", [lambda v: v.scalar_tensor_tensor(
                osl, xsl, 1.0, xsl, ALU.mult, ALU.mult, accum_out=stat[rs, 1:2])],
                reads=xkeys, writes=okeys + SK)
            pr.emit("dve", [lambda v: v.reduce_sum(stat[rs, 0:1], xsl, axis=AX.X)],
                    reads=xkeys, writes=SK)
            pr.emit("dve", [lambda v: v.tensor_scalar(stat[rs, 2:3], stat[rs, 0:1],
                                                      1.0 / H, None, ALU.mult)],
                    reads=SK, writes=SK)
            pr.emit("dve", [lambda v: v.tensor_tensor(stat[rs, 4:5], stat[rs, 2:3],
                                                      stat[rs, 2:3], ALU.mult)],
                    reads=SK, writes=SK)
            pr.emit("dve", [lambda v: v.scalar_tensor_tensor(
                stat[rs, 5:6], stat[rs, 1:2], 1.0 / H, stat[rs, 4:5],
                ALU.mult, ALU.subtract)],
                reads=SK, writes=SK)
            pr.emit("act", [lambda a: a.activation(stat[rs, 6:7], stat[rs, 5:6],
                                                   AF.Sqrt,
                                                   bias=bias_f[rs, 648:649],
                                                   scale=1.0)],
                    reads=SK + EPSK, writes=SK)
            pr.emit("dve", [lambda v: v.reciprocal(stat[rs, 7:8], stat[rs, 6:7])],
                    reads=SK, writes=SK)
            pr.emit("dve", [lambda v: v.tensor_scalar(
                tmsl, xsl, stat[rs, 2:3], stat[rs, 7:8], ALU.subtract, ALU.mult)],
                reads=xkeys + SK, writes=tmkeys)

        ln1_chain(slice(0, 3), xt[0:3, 0:H], bk("xt", 0, H), ot[0:3, 0:H],
                  bk("ot", 0, H), xtm[0:3, 0:H], bk("xtm", 0, H))
        tr_group(0, [(ptb(0)[:, kt * 4:kt * 4 + 3], xtm[0:3, kt * 128:(kt + 1) * 128])
                     for kt in range(KT_H)], bk("xtm", 0, H), np_=3)
        pr.emit("dve", [lambda v: v.tensor_copy(
            xnh_T[:, :].rearrange("p (kt s) -> p kt s", s=3),
            ptb(0)[:, 0:32].rearrange("p (kt s) -> p kt s", s=4)[:, :, 0:3])],
            reads=ptk(0), writes=bk("xnh_T", 0, 24))
        XNH_KEYS = bk("xnh_T", 0, 24)

        def hzc(kt):
            return kt * (NC + 3)

        def tap(name, ap, rows_n, cols, dt, keys):
            if name not in taps:
                return
            tname = "dbg_" + name
            if tname not in tap_outs:
                tap_outs[tname] = nc.dram_tensor(tname, [rows_n, cols], dt,
                                                 kind="ExternalOutput").ap()
            pr.dma("sp", "tp_" + name,
                   lambda h, d=tap_outs[tname], s=ap: h.dma_start(out=d, in_=s),
                   reads=keys, writes=[("OUTD_" + tname, 0)])

        out_keys_all = []
        HZK_ALL = [k for kt in range(KT_I) for k in bk("hz", hzc(kt), hzc(kt) + NC + 3)]
        pending_p9 = [None]

        for c in range(CH):
            t0 = c * NC
            tap_here = (c == tap_chunk)
            NS = 2 if c == 0 else 4

            for tt in range(TT):
                xs = tt % 2
                xsl = xt[:, xs * H:(xs + 1) * H]
                xkeys = bk("xt", xs * H, (xs + 1) * H)
                pr.dma("sp", f"x{xs}", lambda h, r0=t0 + tt * 128, d=xsl:
                       h.dma_start(out=d, in_=x[r0:r0 + 128, :]), writes=xkeys)
                osl = ot[:, xs * H:(xs + 1) * H]
                okeys = bk("ot", xs * H, (xs + 1) * H)
                tsl = xtm[:, xs * H:(xs + 1) * H]
                tkeys = bk("xtm", xs * H, (xs + 1) * H)
                ln1_chain(slice(None), xsl, xkeys, osl, okeys, tsl, tkeys)
                tb = tt % 2
                tr_group(tb, [(ptb(tb)[:, kt * 128:(kt + 1) * 128],
                               tsl[:, kt * 128:(kt + 1) * 128])
                              for kt in range(KT_H)], tkeys)
                dst = xnb[:, :].rearrange("p (kt t) -> p kt t",
                                          t=NC)[:, :, tt * 128:(tt + 1) * 128]
                wkeys = [k for kt in range(KT_H)
                         for k in bk("xnb", kt * NC + tt * 128,
                                     kt * NC + (tt + 1) * 128)]
                pr.emit("dve", [lambda v, tb=tb, dst=dst: v.tensor_copy(
                    dst, ptb(tb)[:, 0:1024].rearrange("p (kt t) -> p kt t", t=128))],
                    reads=ptk(tb), writes=wkeys)

            if pending_p9[0] is not None:
                pending_p9[0]()
                pending_p9[0] = None

            def xn_ap(kt, n0):
                return xnb[:, kt * NC + n0 * 512:kt * NC + n0 * 512 + 512]

            def xn_keys(kt):
                return bk("xnb", kt * NC, kt * NC + NC)

            XNK_ALL = [k for kt in range(KT_H) for k in xn_keys(kt)]

            if tap_here:
                tap("xn", xnb[:, :], 128, KT_H * NC, BF16, bk("xnb", 0, KT_H * NC))

            if c == 0:
                prep_layer(w_inproj, 2 * I2, H, scr_inproj, "scr_inproj", 256)

            if c > 0:
                pr.emit("dve", [lambda v: v.tensor_copy(
                    hz[:, :].rearrange("p (kt q) -> p kt q", q=NC + 3)[:, :, 0:3],
                    htail[:, :].rearrange("p (kt s) -> p kt s", s=3))],
                    reads=bk("htail", 0, KT_I * 3),
                    writes=[k for kt in range(KT_I)
                            for k in bk("hz", hzc(kt), hzc(kt) + 3)])

            nslabs = (2 * I2) // 256
            for s in range(min(NS, nslabs)):
                load_slab("sp", s % NS, 2048, scr_inproj[s, :, :],
                          [("scr_inproj", s)], chan=f"a{s % NS}")
            for s in range(nslabs):
                sc0 = (s % NS) * 2048
                for mmi in range(2):
                    m = 2 * s + mmi
                    bp = (m % 3) * 2
                    groups = []
                    for n in range(NT):
                        pairs = [(gw[:, sc0 + kt * 256 + mmi * 128:
                                     sc0 + kt * 256 + mmi * 128 + 128],
                                  xn_ap(kt, n)) for kt in range(KT_H)]
                        groups.append((pbank(bp + n), pairs))
                    rd = bk("gw", sc0, sc0 + 2048) + XNK_ALL
                    wk = pkeys(bp, NT)
                    if c == 0 and m < KT_I:
                        hp = [(gw[:, sc0 + kt * 256 + mmi * 128:
                                  sc0 + kt * 256 + mmi * 128 + 128],
                               xnh_T[:, kt * 3:(kt + 1) * 3]) for kt in range(KT_H)]
                        hb = ((m % 3) * 2 + 2) % 6
                        groups.append((pbank(hb)[:, 0:3], hp))
                        rd = rd + XNH_KEYS
                        wk = wk + pkeys(hb)
                        mm_emit(groups, rd, wk)
                        d0 = hzc(m)
                        pr.emit("dve", [lambda v, hb=hb, d0=d0: v.tensor_copy(
                            hz[:, d0:d0 + 3], pbank(hb)[:, 0:3])],
                            reads=pkeys(hb), writes=bk("hz", d0, d0 + 3))
                    else:
                        mm_emit(groups, rd, wk)
                    if m < KT_I:
                        d0 = hzc(m) + 3
                        pr.emit("dve", [lambda v, bp=bp, d0=d0: v.tensor_copy(
                            hz[:, d0:d0 + W], pbank(bp, 0, NT))],
                            reads=pkeys(bp, NT), writes=bk("hz", d0, d0 + W))
                    else:
                        mg = m - KT_I
                        d0 = mg * NC
                        pr.emit("act", [lambda a, bp=bp, d0=d0: a.activation(
                            cbuf[:, d0:d0 + W], pbank(bp, 0, NT), AF.Silu)],
                            reads=pkeys(bp, NT), writes=bk("cbuf", d0, d0 + W))
                if s + NS < nslabs:
                    load_slab("sp", (s + NS) % NS, 2048, scr_inproj[s + NS, :, :],
                              [("scr_inproj", s + NS)], chan=f"a{(s + NS) % NS}")
            if tap_here:
                tap("h", hz[:, :], 128, KT_I * (NC + 3), BF16, HZK_ALL)
                tap("gate", cbuf[:, :], 128, KT_I * NC, BF16, bk("cbuf", 0, KT_I * NC))

            if c == 0:
                prep_conv()

            if c + 1 < CH:
                rdk = [k for kt in range(KT_I)
                       for k in bk("hz", hzc(kt) + NC, hzc(kt) + NC + 3)]
                pr.emit("dve", [lambda v: v.tensor_copy(
                    htail[:, :].rearrange("p (kt s) -> p kt s", s=3),
                    hz[:, :].rearrange("p (kt q) -> p kt q",
                                       q=NC + 3)[:, :, NC:NC + 3])],
                    reads=rdk, writes=bk("htail", 0, KT_I * 3))
            for m in range(min(NS, KT_I)):
                load_slab("sp", m % NS, 1024, scr_conv[m, :, :], [("scr_conv", m)],
                          chan=f"a{m % NS}")
            for g in range(NGRP):
                ms = (2 * g, 2 * g + 1)
                for mi, m in enumerate(ms):
                    sc0 = (m % NS) * 2048
                    bp = (m % 3) * 2
                    groups = []
                    for n in range(NT):
                        pairs = []
                        for c2 in range(2):
                            for tp in range(KCONV):
                                j = c2 * KCONV + tp
                                r0 = hzc(2 * g + c2) + n * 512 + tp
                                pairs.append((gw[:, sc0 + j * 128:sc0 + (j + 1) * 128],
                                              hz[:, r0:r0 + 512]))
                        pairs.append((convbrow[0:1, m * 128:(m + 1) * 128],
                                      ONES_ROW512))
                        groups.append((pbank(bp + n), pairs))
                    rd = bk("gw", sc0, sc0 + 1024)
                    rd += bk("hz", hzc(2 * g), hzc(2 * g) + NC + 3)
                    rd += bk("hz", hzc(2 * g + 1), hzc(2 * g + 1) + NC + 3)
                    rd += bk("convbrow", m * 128, (m + 1) * 128) + list(ONES_KEYS)
                    mm_emit(groups, rd, pkeys(bp, NT))
                for mi, m in enumerate(ms):
                    bp = (m % 3) * 2
                    g0 = m * NC
                    d0 = hzc(m) + 3
                    pr.emit("dve", [lambda v, bp=bp, d0=d0, g0=g0: v.tensor_tensor(
                        hz[:, d0:d0 + W], pbank(bp, 0, NT), cbuf[:, g0:g0 + W],
                        ALU.mult)],
                        reads=pkeys(bp, NT) + bk("cbuf", g0, g0 + W),
                        writes=bk("hz", d0, d0 + W))
                for mn in (2 * g + NS, 2 * g + NS + 1):
                    if mn < KT_I:
                        load_slab("sp", mn % NS, 1024, scr_conv[mn, :, :],
                                  [("scr_conv", mn)], chan=f"a{mn % NS}")
            if tap_here:
                tap("h2", hz[:, :], 128, KT_I * (NC + 3), BF16, HZK_ALL)

            def do_p7a():
                WGK = bk("wgres", 0, KT_H * 8)
                BGRK = bk("bgate_row", 0, 8)
                GFK = bk("gout_fm", 0, NC)
                groups = []
                for n in range(NT):
                    pairs = [(wgres[:, kt * 8:(kt + 1) * 8], xn_ap(kt, n))
                             for kt in range(KT_H)]
                    pairs.append((bgate_row[0:1, :], ONES_ROW512))
                    groups.append((pbank(4 + n)[0:8, :], pairs))
                mm_emit(groups, list(WGK) + list(BGRK) + list(ONES_KEYS) + XNK_ALL,
                        pkeys(4, NT))
                pr.emit("dve", [lambda v: v.tensor_copy(
                    gout_fm[0:8, 0:W], pbank(4, 0, NT)[0:8, :])],
                    reads=pkeys(4, NT), writes=GFK)
                tr_group(0, [(ptb(0)[:, tt * 8:(tt + 1) * 8],
                              gout_fm[0:8, tt * 128:(tt + 1) * 128])
                             for tt in range(TT)], GFK, np_=8)
                pr.emit("dve", [lambda v: v.tensor_copy(
                    gtw[:, 0:8 * TT], ptb(0)[:, 0:8 * TT])],
                    reads=ptk(0), writes=bk("gtw", 0, 8 * TT))
                GTK = bk("gtw", 0, 8 * TT)
                GOK = bk("gops", 0, 32)
                GSK = bk("gsc", 0, 8)
                CTK = bk("combtm", 0, 8)
                for tt in range(TT):
                    gsl = gtw[:, tt * 8:(tt + 1) * 8]
                    pr.emit("dve", [lambda v, gsl=gsl: v.reduce_max(
                        gsc[:, 0:1], gsl, axis=AX.X)], reads=GTK, writes=GSK)
                    pr.emit("dve", [lambda v, gsl=gsl: v.tensor_scalar(
                        gops[:, 0:8], gsl, gsc[:, 0:1], None, ALU.is_ge)],
                        reads=GTK + GSK, writes=GOK)
                    pr.emit("dve", [lambda v, gsl=gsl: v.scalar_tensor_tensor(
                        gops[:, 8:16], gops[:, 0:8], -1e30, gsl, ALU.mult, ALU.add)],
                        reads=GOK + GTK, writes=GOK)
                    pr.emit("dve", [lambda v: v.reduce_max(gsc[:, 1:2], gops[:, 8:16],
                                                           axis=AX.X)],
                            reads=GOK, writes=GSK)
                    pr.emit("dve", [lambda v: v.tensor_scalar(gops[:, 16:24],
                                                              gops[:, 8:16],
                                                              gsc[:, 1:2], None,
                                                              ALU.is_ge)],
                            reads=GOK + GSK, writes=GOK)
                    pr.emit("dve", [lambda v: v.tensor_scalar(gsc[:, 2:3],
                                                              gsc[:, 1:2], -1.0, None,
                                                              ALU.mult)],
                            reads=GSK, writes=GSK)
                    pr.emit("act", [lambda a: a.activation(gsc[:, 3:4], gsc[:, 0:1],
                                                           AF.Sigmoid,
                                                           bias=gsc[:, 2:3],
                                                           scale=1.0)],
                            reads=GSK, writes=GSK)
                    pr.emit("dve", [lambda v: v.tensor_scalar(gsc[:, 4:5],
                                                              gsc[:, 3:4], -1.0, 1.0,
                                                              ALU.mult, ALU.add)],
                            reads=GSK, writes=GSK)
                    pr.emit("dve", [lambda v: v.tensor_scalar(gops[:, 24:32],
                                                              gops[:, 0:8],
                                                              gsc[:, 3:4], None,
                                                              ALU.mult)],
                            reads=GOK + GSK, writes=GOK)
                    pr.emit("dve", [lambda v: v.scalar_tensor_tensor(
                        combtm[:, :], gops[:, 16:24], gsc[:, 4:5], gops[:, 24:32],
                        ALU.mult, ALU.add)],
                        reads=GOK + GSK, writes=CTK)
                    tr_group(1, [(ptb(1)[0:8, tt * 128:(tt + 1) * 128],
                                  combtm[:, 0:8])], CTK)
                pr.emit("dve", [lambda v: v.tensor_copy(comb8[0:8, :],
                                                        ptb(1)[0:8, 0:NC])],
                        reads=ptk(1), writes=bk("comb8", 0, NC))

            do_p7a()

            if c == 0:
                prep_layer(w_fc, H, I2, scr_fc, "scr_fc", 128)

            RK = bk("rows", 0, 2 * NC)
            RBK = bk("rows_bf", 0, NC)
            H2K = [k for kt in range(KT_I)
                   for k in bk("hz", hzc(kt) + 3, hzc(kt) + 3 + NC)]
            for n in range(NT):
                sqk = []
                for kt in range(KT_I):
                    d0 = hzc(kt) + 3 + n * 512
                    f0 = kt * 512
                    pr.emit("act", [lambda a, d0=d0, f0=f0: a.activation(
                        fbuf[:, f0:f0 + 512], hz[:, d0:d0 + 512], AF.Square)],
                        reads=bk("hz", d0, d0 + 512), writes=bk("fbuf", f0, f0 + 512))
                    sqk += bk("fbuf", f0, f0 + 512)
                groups = [
                    (pbank(4 + n)[0:1, :],
                     [(ONES_COL, hz[:, hzc(kt) + 3 + n * 512:
                                  hzc(kt) + 3 + n * 512 + 512])
                      for kt in range(KT_I)]),
                    (pbank(n)[0:1, :],
                     [(ONES_COL, fbuf[:, kt * 512:(kt + 1) * 512])
                      for kt in range(KT_I)]),
                ]
                mm_emit(groups, list(ONES_KEYS) + H2K + sqk, pkeys(4 + n) + pkeys(n))
            pr.emit("dve", [lambda v: v.tensor_scalar(rows[0:1, 0:W],
                                                      pbank(4, 0, NT)[0:1, :],
                                                      1.0 / I2, None, ALU.mult)],
                    reads=pkeys(4, NT), writes=RK)
            pr.emit("dve", [lambda v: v.tensor_scalar(rows[0:1, NC:NC + W],
                                                      pbank(0, 0, NT)[0:1, :],
                                                      1.0 / I2, None, ALU.mult)],
                    reads=pkeys(0, NT), writes=RK)
            e3, e3k = escr_slot()
            pr.emit("dve", [lambda v, e3=e3: v.tensor_tensor(
                e3[0:1, 0:W], rows[0:1, 0:W], rows[0:1, 0:W], ALU.mult)],
                reads=RK, writes=e3k)
            pr.emit("dve", [lambda v, e3=e3: v.tensor_tensor(
                rows[0:1, NC:NC + W], rows[0:1, NC:NC + W], e3[0:1, 0:W],
                ALU.subtract)],
                reads=RK + e3k, writes=RK)
            pr.emit("act", [lambda a: a.activation(rows[0:1, NC:NC + W],
                                                   rows[0:1, NC:NC + W], AF.Sqrt,
                                                   bias=bias_f[0:1, 648:649],
                                                   scale=1.0)],
                    reads=RK + EPSK, writes=RK)
            pr.emit("dve", [lambda v: v.reciprocal(rows[0:1, NC:NC + W],
                                                   rows[0:1, NC:NC + W])],
                    reads=RK, writes=RK)
            pr.emit("dve", [lambda v: v.tensor_tensor(rows[0:1, 0:W],
                                                      rows[0:1, 0:W],
                                                      rows[0:1, NC:NC + W],
                                                      ALU.mult)],
                    reads=RK, writes=RK)
            pr.emit("dve", [lambda v: v.tensor_copy(rows_bf[0:1, 0:W],
                                                    rows[0:1, NC:NC + W])],
                    reads=RK, writes=RBK)
            groups = [(pbank(2 + n),
                       [(ONES_ROW128, rows_bf[0:1, n * 512:(n + 1) * 512])])
                      for n in range(NT)]
            mm_emit(groups, RBK + list(ONES_KEYS), pkeys(2, NT))
            pr.emit("dve", [lambda v: v.tensor_copy(rmb[:, 0:W], pbank(2, 0, NT))],
                    reads=pkeys(2, NT), writes=bk("rmb", 0, NC))
            pr.emit("dve", [lambda v: v.tensor_copy(rows_bf[0:1, 0:W],
                                                    rows[0:1, 0:W])],
                    reads=RK, writes=RBK)
            groups = [(pbank(2 + n),
                       [(ONES_ROW128, rows_bf[0:1, n * 512:(n + 1) * 512])])
                      for n in range(NT)]
            mm_emit(groups, RBK + list(ONES_KEYS), pkeys(2, NT))
            pr.emit("dve", [lambda v: v.tensor_copy(rmb[:, NC:NC + W],
                                                    pbank(2, 0, NT))],
                    reads=pkeys(2, NT), writes=bk("rmb", NC, 2 * NC))
            RMK = bk("rmb", 0, NC)
            MMK = bk("rmb", NC, 2 * NC)
            for kt in range(KT_I):
                z0 = hzc(kt) + 3
                d0 = kt * NC
                e1, e1k = escr_slot()
                e2, e2k = escr_slot()
                ga, gak = bias_ap(LNGC(kt))
                ba, bak = bias_ap(LNBC(kt))
                pr.emit("pool", [lambda v, z0=z0, e1=e1: v.tensor_tensor(
                    e1, hz[:, z0:z0 + W], rmb[:, 0:W], ALU.mult)],
                    reads=bk("hz", z0, z0 + W) + RMK, writes=e1k)
                pr.emit("pool", [lambda v, e1=e1, e2=e2: v.tensor_tensor(
                    e2, e1, rmb[:, NC:NC + W], ALU.subtract)],
                    reads=e1k + MMK, writes=e2k)
                pr.emit("dve", [lambda v, e2=e2, d0=d0, ga=ga, ba=ba: v.tensor_scalar(
                    cbuf[:, d0:d0 + W], e2, ga, ba, ALU.mult, ALU.add)],
                    reads=e2k + gak + bak, writes=bk("cbuf", d0, d0 + W))
            if tap_here:
                tap("hn", cbuf[:, :], 128, KT_I * NC, BF16, bk("cbuf", 0, KT_I * NC))

            if c == 0:
                prep_layer(w_cproj, I2, H, scr_cproj, "scr_cproj", 256)

            CBK_ALL = [k for kt in range(KT_I)
                       for k in bk("cbuf", kt * NC, kt * NC + NC)]
            for s in range(min(NS, KT_H)):
                load_slab("sp", s % NS, 2048, scr_fc[s, :, :], [("scr_fc", s)],
                          chan=f"a{s % NS}")
            for m in range(KT_H):
                sc0 = (m % NS) * 2048
                bp = (m % 3) * 2
                groups = [(pbank(bp + n),
                           [(gw[:, sc0 + kt * 128:sc0 + (kt + 1) * 128],
                             cbuf[:, kt * NC + n * 512:kt * NC + n * 512 + 512])
                            for kt in range(KT_I)]) for n in range(NT)]
                mm_emit(groups, bk("gw", sc0, sc0 + 2048) + CBK_ALL, pkeys(bp, NT))
                fa, fak = bias_ap(FCBC(m))
                d0 = m * NC
                pr.emit("act", [lambda a, bp=bp, d0=d0, fa=fa: a.activation(
                    fbuf[:, d0:d0 + W], pbank(bp, 0, NT), AF.Silu, bias=fa,
                    scale=1.0)],
                    reads=pkeys(bp, NT) + fak, writes=bk("fbuf", d0, d0 + W))
                if m + NS < KT_H:
                    load_slab("sp", (m + NS) % NS, 2048, scr_fc[m + NS, :, :],
                              [("scr_fc", m + NS)], chan=f"a{(m + NS) % NS}")
            if tap_here:
                tap("u", fbuf[:, 0:KT_H * NC], 128, KT_H * NC, BF16,
                    bk("fbuf", 0, KT_H * NC))

            if c == 0:
                prep_layer(w_out, H, I2, scr_outw, "scr_outw", 128)

            UK_ALL = [k for kt in range(KT_H)
                      for k in bk("fbuf", kt * NC, kt * NC + NC)]
            for s in range(min(NS, 8)):
                load_slab("sp", s % NS, 2048, scr_cproj[s, :, :],
                          [("scr_cproj", s)], chan=f"a{s % NS}")
            for s in range(8):
                sc0 = (s % NS) * 2048
                for mmi in range(2):
                    m = 2 * s + mmi
                    bp = (m % 3) * 2
                    groups = [(pbank(bp + n),
                               [(gw[:, sc0 + kt * 256 + mmi * 128:
                                    sc0 + kt * 256 + mmi * 128 + 128],
                                 fbuf[:, kt * NC + n * 512:kt * NC + n * 512 + 512])
                                for kt in range(KT_H)]) for n in range(NT)]
                    mm_emit(groups, bk("gw", sc0, sc0 + 2048) + UK_ALL, pkeys(bp, NT))
                    ca, cak = bias_ap(CPROJBC(m))
                    d0 = m * NC
                    z0 = hzc(m) + 3
                    pr.emit("dve", [lambda v, bp=bp, d0=d0, z0=z0, ca=ca:
                                    v.scalar_tensor_tensor(
                        cbuf[:, d0:d0 + W], hz[:, z0:z0 + W], ca,
                        pbank(bp, 0, NT), ALU.add, ALU.add)],
                        reads=pkeys(bp, NT) + bk("hz", z0, z0 + W) + cak,
                        writes=bk("cbuf", d0, d0 + W))
                if s + NS < 8:
                    load_slab("sp", (s + NS) % NS, 2048, scr_cproj[s + NS, :, :],
                              [("scr_cproj", s + NS)], chan=f"a{(s + NS) % NS}")
            if tap_here:
                tap("hf", cbuf[:, :], 128, KT_I * NC, BF16, bk("cbuf", 0, KT_I * NC))

            for s in range(min(NS, KT_H)):
                load_slab("sp", s % NS, 2048, scr_outw[s, :, :], [("scr_outw", s)],
                          chan=f"a{s % NS}")
            for m in range(KT_H):
                sc0 = (m % NS) * 2048
                bp = (m % 3) * 2
                groups = [(pbank(bp + n),
                           [(gw[:, sc0 + kt * 128:sc0 + (kt + 1) * 128],
                             cbuf[:, kt * NC + n * 512:kt * NC + n * 512 + 512])
                            for kt in range(KT_I)]) for n in range(NT)]
                mm_emit(groups, bk("gw", sc0, sc0 + 2048) + CBK_ALL, pkeys(bp, NT))
                d0 = m * NC
                pr.emit("dve", [lambda v, bp=bp, d0=d0: v.tensor_copy(
                    sb_s[:, d0:d0 + W], pbank(bp, 0, NT))],
                    reads=pkeys(bp, NT), writes=bk("sb_s", d0, d0 + W))
                if m + NS < KT_H:
                    load_slab("sp", (m + NS) % NS, 2048, scr_outw[m + NS, :, :],
                              [("scr_outw", m + NS)], chan=f"a{(m + NS) % NS}")
            if tap_here:
                tap("s", sb_s[:, :], 128, KT_H * NC, BF16, bk("sb_s", 0, KT_H * NC))

            for e in range(EXP):
                bp = (e % 2) * 2
                groups = [(pbank(bp + n),
                           [(selb[0:8, e * 128:(e + 1) * 128],
                             comb8[0:8, n * 512:(n + 1) * 512])]) for n in range(NT)]
                mm_emit(groups,
                        bk("comb8", 0, NC) + bk("selb", e * 128, (e + 1) * 128),
                        pkeys(bp, NT))
                r0 = e * NC
                pr.emit("dve", [lambda v, bp=bp, r0=r0: v.tensor_copy(
                    fbuf[:, r0:r0 + W], pbank(bp, 0, NT))],
                    reads=pkeys(bp, NT), writes=bk("fbuf", r0, r0 + W))
            if tap_here:
                tap("gout", gout_fm[:, :], 8, NC, BF16, bk("gout_fm", 0, NC))
                tap("cb", fbuf[:, 0:EXP * NC], 128, EXP * NC, BF16,
                    bk("fbuf", 0, EXP * NC))

            idx12 = lambda e, wi, s: (e * 2 + wi) * 8 + s
            idxgl = lambda e, wi, m: (e * 2 + wi) * 16 + m
            idx3 = lambda e, m: e * 8 + m
            ZK_ALL = [k for kt in range(KT_I)
                      for k in bk("hz", hzc(kt) + 3, hzc(kt) + 3 + NC)]
            for e in range(EXP):
                def get12(slot, wi, sidx, e=e, c=c):
                    if c == 0:
                        src = (w1 if wi == 0 else w2)[e, :,
                                                      sidx * 256:(sidx + 1) * 256] \
                            .rearrange("(kt p) m -> p kt m", p=128)
                        convert_into_slot(slot, src, scr_w12, "scr_w12",
                                          idx12(e, wi, sidx))
                    else:
                        load_slab("sp", slot, 2048,
                                  scr_w12[idx12(e, wi, sidx), :, :],
                                  [("scr_w12", idx12(e, wi, sidx))])

                def getgl(slot, wi, midx, e=e, c=c):
                    if c == 0:
                        src = (wg if wi == 0 else wl)[e, :,
                                                      midx * 128:(midx + 1) * 128] \
                            .rearrange("(kt p) m -> p kt m", p=128)
                        convert_into_slot(slot, src, scr_wgl, "scr_wgl",
                                          idxgl(e, wi, midx))
                    else:
                        load_slab("sp", slot, 2048,
                                  scr_wgl[idxgl(e, wi, midx), :, :],
                                  [("scr_wgl", idxgl(e, wi, midx))])

                def get3(slot, midx, e=e, c=c):
                    if c == 0:
                        src = w3[e, :, midx * 128:(midx + 1) * 128] \
                            .rearrange("(kt p) m -> p kt m", p=128)
                        convert_into_slot(slot, src, scr_w3, "scr_w3", idx3(e, midx))
                    else:
                        load_slab("sp", slot, 2048, scr_w3[idx3(e, midx), :, :],
                                  [("scr_w3", idx3(e, midx))])

                for s in range(2):
                    get12(s % 2, 0, s)
                    get12(2 + s % 2, 1, s)
                for s in range(8):
                    scA = (s % 2) * 2048
                    scB = (2 + s % 2) * 2048
                    for mmi in range(2):
                        m = 2 * s + mmi
                        bpA = 0 if m % 2 == 0 else 4
                        groups = []
                        for n in range(NT):
                            groups.append((pbank(bpA + n),
                                           [(gw[:, scA + kt * 256 + mmi * 128:
                                                scA + kt * 256 + mmi * 128 + 128],
                                             xn_ap(kt, n)) for kt in range(KT_H)]))
                        for n in range(NT):
                            groups.append((pbank(2 + n),
                                           [(gw[:, scB + kt * 256 + mmi * 128:
                                                scB + kt * 256 + mmi * 128 + 128],
                                             xn_ap(kt, n)) for kt in range(KT_H)]))
                        rd = bk("gw", scA, scA + 2048) + bk("gw", scB, scB + 2048) \
                            + XNK_ALL
                        mm_emit(groups, rd, pkeys(bpA, NT) + pkeys(2, NT))
                        e1, e1k = escr_slot()
                        e2, e2k = escr_slot()
                        b1a, b1k = bias_ap(B1C(e, m))
                        b2a, b2k = bias_ap(B2C(e, m))
                        pr.emit("dve", [lambda v, bpA=bpA, e1=e1, b1a=b1a:
                                        v.tensor_scalar(e1, pbank(bpA, 0, NT), b1a,
                                                        None, ALU.add)],
                                reads=pkeys(bpA, NT) + b1k, writes=e1k)
                        pr.emit("dve", [lambda v, e2=e2, b2a=b2a:
                                        v.tensor_scalar(e2, pbank(2, 0, NT), b2a,
                                                        None, ALU.add)],
                                reads=pkeys(2, NT) + b2k, writes=e2k)
                        d0 = hzc(m) + 3
                        pr.emit("dve", [lambda v, e1=e1, e2=e2, d0=d0:
                                        v.tensor_tensor(hz[:, d0:d0 + W], e1, e2,
                                                        ALU.mult)],
                                reads=e1k + e2k, writes=bk("hz", d0, d0 + W))
                    if s + 2 < 8:
                        get12((s + 2) % 2, 0, s + 2)
                        get12(2 + (s + 2) % 2, 1, s + 2)
                if tap_here and e == tap_expert:
                    tap("z", hz[:, :], 128, KT_I * (NC + 3), BF16, HZK_ALL)

                for s in range(2):
                    getgl(s % 2, 0, s)
                    getgl(2 + s % 2, 1, s)
                for m in range(KT_I):
                    scG = (m % 2) * 2048
                    scL = (2 + m % 2) * 2048
                    bpG = 0 if m % 2 == 0 else 4
                    groups = []
                    for n in range(NT):
                        groups.append((pbank(bpG + n),
                                       [(gw[:, scG + kt * 128:scG + (kt + 1) * 128],
                                         hz[:, hzc(kt) + 3 + n * 512:
                                            hzc(kt) + 3 + n * 512 + 512])
                                        for kt in range(KT_I)]))
                    for n in range(NT):
                        groups.append((pbank(2 + n),
                                       [(gw[:, scL + kt * 128:scL + (kt + 1) * 128],
                                         hz[:, hzc(kt) + 3 + n * 512:
                                            hzc(kt) + 3 + n * 512 + 512])
                                        for kt in range(KT_I)]))
                    rd = bk("gw", scG, scG + 2048) + bk("gw", scL, scL + 2048) + ZK_ALL
                    mm_emit(groups, rd, pkeys(bpG, NT) + pkeys(2, NT))
                    sg, sgk = escr_slot()
                    tl, tlk = escr_slot()
                    bga, bgk = bias_ap(BGC(e, m))
                    bla, blk = bias_ap(BLC(e, m))
                    pr.emit("act", [lambda a, bpG=bpG, sg=sg, bga=bga: a.activation(
                        sg, pbank(bpG, 0, NT), AF.Silu, bias=bga, scale=1.0)],
                        reads=pkeys(bpG, NT) + bgk, writes=sgk)
                    pr.emit("dve", [lambda v, tl=tl, bla=bla: v.tensor_scalar(
                        tl, pbank(2, 0, NT), bla, None, ALU.add)],
                        reads=pkeys(2, NT) + blk, writes=tlk)
                    d0 = m * NC
                    pr.emit("dve", [lambda v, sg=sg, tl=tl, d0=d0: v.tensor_tensor(
                        cbuf[:, d0:d0 + W], sg, tl, ALU.mult)],
                        reads=sgk + tlk, writes=bk("cbuf", d0, d0 + W))
                    if m + 2 < KT_I:
                        getgl((m + 2) % 2, 0, m + 2)
                        getgl(2 + (m + 2) % 2, 1, m + 2)
                if tap_here and e == tap_expert:
                    tap("o", cbuf[:, :], 128, KT_I * NC, BF16,
                        bk("cbuf", 0, KT_I * NC))

                for s in range(2):
                    get3(s % 2, s)
                for m in range(KT_H):
                    sc0 = (m % 2) * 2048
                    bp = (m % 3) * 2
                    groups = [(pbank(bp + n),
                               [(gw[:, sc0 + kt * 128:sc0 + (kt + 1) * 128],
                                 cbuf[:, kt * NC + n * 512:kt * NC + n * 512 + 512])
                                for kt in range(KT_I)]) for n in range(NT)]
                    mm_emit(groups, bk("gw", sc0, sc0 + 2048) + CBK_ALL,
                            pkeys(bp, NT))
                    ty, tyk = escr_slot()
                    b3a, b3k = bias_ap(B3C(e, m))
                    pr.emit("dve", [lambda v, bp=bp, ty=ty, b3a=b3a: v.tensor_scalar(
                        ty, pbank(bp, 0, NT), b3a, None, ALU.add)],
                        reads=pkeys(bp, NT) + b3k, writes=tyk)
                    cb0 = e * NC
                    cbk2 = bk("fbuf", cb0, cb0 + W)
                    d0 = m * NC
                    ybk = bk("yb", d0, d0 + W)
                    if e == 0:
                        pr.emit("dve", [lambda v, ty=ty, cb0=cb0, d0=d0:
                                        v.tensor_tensor(yb[:, d0:d0 + W], ty,
                                                        fbuf[:, cb0:cb0 + W],
                                                        ALU.mult)],
                                reads=tyk + cbk2, writes=ybk)
                    else:
                        t2, t2k = escr_slot()
                        pr.emit("dve", [lambda v, ty=ty, cb0=cb0, t2=t2:
                                        v.tensor_tensor(t2, ty, fbuf[:, cb0:cb0 + W],
                                                        ALU.mult)],
                                reads=tyk + cbk2, writes=t2k)
                        pr.emit("dve", [lambda v, t2=t2, d0=d0: v.tensor_tensor(
                            yb[:, d0:d0 + W], yb[:, d0:d0 + W], t2, ALU.add)],
                            reads=t2k + ybk, writes=ybk)
                    if m + 2 < KT_H:
                        get3((m + 2) % 2, m + 2)
            if tap_here:
                tap("y", yb[:, :], 128, KT_H * NC, BF16, bk("yb", 0, KT_H * NC))

            def do_p9(t0=t0):
                for kt in range(KT_H):
                    d0 = kt * NC
                    e1, e1k = escr_slot()
                    pr.emit("act", [lambda a, d0=d0, e1=e1: a.activation(
                        e1, yb[:, d0:d0 + W], AF.Silu)],
                        reads=bk("yb", d0, d0 + W), writes=e1k)
                    pr.emit("pool", [lambda v, e1=e1, d0=d0: v.tensor_tensor(
                        cbuf[:, d0:d0 + W], e1, sb_s[:, d0:d0 + W], ALU.mult)],
                        reads=e1k + bk("sb_s", d0, d0 + W),
                        writes=bk("cbuf", d0, d0 + W))
                for tt in range(TT):
                    tb = tt % 2
                    fk = []
                    trs = []
                    for kt in range(KT_H):
                        c0 = kt * NC + tt * 128
                        trs.append((ptb(tb)[:, kt * 128:(kt + 1) * 128],
                                    cbuf[:, c0:c0 + 128]))
                        fk += bk("cbuf", c0, c0 + 128)
                    tr_group(tb, trs, fk)
                    xs = tt % 2
                    xsl = xt[:, xs * H:(xs + 1) * H]
                    xkeys = bk("xt", xs * H, (xs + 1) * H)
                    pr.dma("sp", f"x{xs}", lambda h, r0=t0 + tt * 128, d=xsl:
                           h.dma_start(out=d, in_=x[r0:r0 + 128, :]), writes=xkeys)
                    osl = ot[:, xs * H:(xs + 1) * H]
                    okeys = bk("ot", xs * H, (xs + 1) * H)
                    pr.emit("dve", [lambda v, tb=tb, osl=osl, xsl=xsl: v.tensor_tensor(
                        osl, ptb(tb)[:, 0:1024], xsl, ALU.add)],
                        reads=ptk(tb) + xkeys, writes=okeys)
                    pr.dma("sp", f"o{xs}", lambda h, r0=t0 + tt * 128, sl=osl:
                           h.dma_start(out=out[r0:r0 + 128, :], in_=sl),
                           reads=okeys, writes=[("OUT", (t0 + tt * 128) // 512)])
                    out_keys_all.append(("OUT", (t0 + tt * 128) // 512))
            if c + 1 < CH:
                pending_p9[0] = do_p9
            else:
                do_p9()

        def _final(h):
            for q, n in sorted(pr.qcnt.items()):
                h.wait_ge(pr.sems[q], n * 16)
            return h.nop()
        pr.emit("sp", [_final], reads=list(set(out_keys_all)))
        pr.flush(nc, ctx)
    return nc


def _core_inputs(inputs, core, T=4096):
    x = inputs["L0_attn_hidden_states"]
    B, L, Hm = x.shape
    b = (core * T) // L
    s0 = (core * T) % L
    xs = np.ascontiguousarray(x[b, s0:s0 + T, :])
    if s0 == 0:
        xhh = np.zeros((3, Hm), dtype=np.float32)
    else:
        xhh = np.ascontiguousarray(x[b, s0 - 3:s0, :])
    m = {"x": xs, "xh": xhh}
    for k, v in inputs.items():
        if k == "L0_attn_hidden_states":
            continue
        m[k] = np.ascontiguousarray(v)
    m["onesf"] = np.ones((128, 512), dtype=np.float32)
    m["identf"] = np.eye(128, dtype=np.float32)
    sel = np.zeros((8, 1024), dtype=np.float32)
    for e in range(8):
        sel[e, e * 128:(e + 1) * 128] = 1.0
    m["sel8"] = sel
    return m


def kernel(**inputs):
    x = inputs["L0_attn_hidden_states"]
    B, L, Hm = x.shape
    n_cores = 8
    T = (B * L) // n_cores
    nc = build_program(T=T, NC=1024)
    in_maps = [_core_inputs(inputs, c, T) for c in range(n_cores)]
    res = run_bass_kernel_spmd(nc, in_maps, core_ids=list(range(n_cores)))
    outs = [res.results[c]["out"] for c in range(n_cores)]
    return np.concatenate(outs, axis=0).reshape(B, L, Hm)
